# version 01
# baseline (speedup 1.0000x reference)
import math
import numpy as np
import ml_dtypes
from contextlib import ExitStack

import concourse.bass as bass
import concourse.mybir as mybir
from concourse.bass_utils import run_bass_kernel_spmd

F32 = mybir.dt.float32
BF16 = mybir.dt.bfloat16
I32 = mybir.dt.int32
AF = mybir.ActivationFunctionType
ALU = mybir.AluOpType
AX = mybir.AxisListType
BF = ml_dtypes.bfloat16

T, D = 8192, 2048
R, NSUB, KB, NB = 1024, 8, 1024, 8
H, DN, DR, DQK, DV = 16, 128, 64, 192, 128
CQ, CK, DFF, HPH = 1536, 512, 8192, 8
SCALE_QK = 1.0 / math.sqrt(DQK)
EXP_SHIFT = 60.0

ENGS = ('gpsimd', 'sync', 'tensor', 'vector', 'scalar')
DBG_ONLY_ADD = None
DBG_NB = None
G, SY, TE, VE, SC = ENGS
PHASE_LIMIT = 99
DEBUG_TAPS = ()

(C_SQK, C_EPSR, C_I512, C_EPSLN, C_ID, C_B512, C_I1536, C_M128, C_ONE,
 C_NEG1, C_SHIFT) = range(11)

AR = 45056
A_KROT = 0
A_QFT = 4096
A_OTN = 16384
FR = 20480


def win(ap, start, n, stride, width):
    if n == 1:
        return ap[:, start:start + width].rearrange("p (a b) -> p a b", a=1)
    sub = ap[:, start:start + n * stride]
    return sub.rearrange("p (a b) -> p a b", b=stride)[:, :, 0:width]


class Sem:
    def __init__(self, es, nc, name):
        self.h = es.enter_context(nc.semaphore(name))
        self.c = 0


class Ctx:
    def __init__(self, nc, cur):
        self.nc, self.cur = nc, cur

    def _fin(self, ins, inc):
        if inc is None:
            return
        sem, n = inc if isinstance(inc, tuple) else (inc, 1)
        if ins is not None:
            ins.then_inc(sem.h, n)
        sem.c += n

    def op(self, engine, fn, inc=None):
        ins = fn(getattr(self.nc, engine)) if self.cur == engine else None
        self._fin(ins, inc)

    def wait(self, engine, sem, val):
        if self.cur == engine and val > 0:
            getattr(self.nc, engine).wait_ge(sem.h, val)

    def drain(self, engine):
        if self.cur == engine:
            getattr(self.nc, engine).drain()

    def mm(self, out, lhsT, rhs, start, stop, inc=None):
        self.op('tensor', lambda e: e.matmul(out, lhsT, rhs, start=start,
                                             stop=stop), inc=inc)

    def dma(self, engine, out, in_, inc=None, transpose=False, accum=False):
        def f(e):
            if accum:
                return e.dma_start(out=out, in_=in_, accum_op=ALU.add)
            return e.dma_start(out=out, in_=in_, transpose=transpose)
        if inc is None:
            inc = self.dsem[engine]
        self.op(engine, f, inc=(inc, 16) if isinstance(inc, Sem) else inc)

    def barrier(self, sem):
        base = sem.c
        for e in ENGS:
            self.op(e, lambda eng: eng.nop(), inc=sem)
        for e in ENGS:
            self.wait(e, sem, base + len(ENGS))


class NS:
    pass


def build(debug_taps=()):
    nc = bass.Bass("TRN2", target_bir_lowering=False, debug=False,
                   detect_race_conditions=False)
    es = ExitStack()
    es.__enter__()
    B = NS()
    B.nc, B.es = nc, es

    din = lambda n, sh, dt_: nc.dram_tensor(n, sh, dt_,
                                            kind="ExternalInput").ap()
    dscr = lambda n, sh, dt_: nc.dram_tensor(n, sh, dt_, kind="Internal").ap()
    B.xq_d = din("xq", [R, D], F32)
    B.xk_d = din("xk", [T, D], F32)
    B.wqa_d = din("wqa", [D, CQ], F32)
    B.wqb_d = din("wqb", [CQ, H * DQK], F32)
    B.qnw_d = din("qnw", [128, 12], F32)
    B.wkva_d = din("wkva", [D, CK], F32)
    B.wkvb_d = din("wkvb", [CK, H * 256], F32)
    B.kvnw_d = din("kvnw", [128, 4], F32)
    B.wkr_d = din("wkr", [D, DR], F32)
    B.wo_d = din("wo", [D, D], F32)
    B.gw_d = din("gw", [DFF, D], I32)
    B.uw_d = din("uw", [DFF, D], I32)
    B.dw_d = din("dw", [D, DFF], I32)
    B.gsc_d = din("gsc", [128, 64], F32)
    B.usc_d = din("usc", [128, 64], F32)
    B.dsc_d = din("dsc", [128, 16], F32)
    B.ckt_d = din("ckt", [T, 32], F32)
    B.skt_d = din("skt", [T, 32], F32)
    B.cqr_d = din("cqr", [R, H * 32], F32)
    B.sqr_d = din("sqr", [R, H * 32], F32)
    B.mask_d = din("mask01", [128, KB], F32)
    B.cst_d = din("cst", [128, 16], F32)
    B.oneh_d = din("oneh", [128, 256], F32)
    B.out_d = nc.dram_tensor("out", [R, D], F32, kind="ExternalOutput").ap()

    B.ckvT_d = dscr("ckvT_scr", [128, 4 * T], BF16)
    B.x2_d = dscr("x2_scr", [128, NSUB * D], F32)
    B.yffn_d = dscr("yffn_scr", [128, NSUB * D], F32)
    B.otn0_d = dscr("otn0_scr", [128, HPH * R], BF16)

    B.taps = {}
    for name, shape, dt_ in debug_taps:
        B.taps[name] = nc.dram_tensor("tap_" + name, shape, dt_,
                                      kind="ExternalOutput").ap()

    arena_t = es.enter_context(nc.sbuf_tensor("arena", [128, AR], F32))
    B.bufF = lambda c0, w: arena_t[:, c0:c0 + w]
    B.bufB = lambda c0, w: arena_t[:, c0:c0 + (w + 1) // 2].bitcast(
        BF16)[:, 0:w]

    sbt = lambda n, sh, dt_: es.enter_context(
        nc.sbuf_tensor(n, sh, dt_))[:, :]
    B.cst = sbt("s_cst", [128, 16], F32)
    B.qnw = sbt("s_qnw", [128, 12], F32)
    B.kvnw = sbt("s_kvnw", [128, 4], F32)
    B.gsc = sbt("s_gsc", [128, 64], F32); B.gsc2 = sbt("gsc2", [128, 64], F32)
    B.usc = sbt("s_usc", [128, 64], F32); B.usc2 = sbt("usc2", [128, 64], F32)
    B.dsc = sbt("s_dsc", [128, 16], F32); B.dsc2 = sbt("dsc2", [128, 16], F32)
    B.mask01 = sbt("mask01s", [128, KB], BF16)
    B.ones_row = sbt("ones_row", [1, 128], F32)
    B.ones_bfc = sbt("ones_bfc", [128, 1], BF16)
    B.stats = sbt("stats", [128, 128], F32)
    B.statB = sbt("statB", [128, 128], F32)
    B.statC = sbt("statC", [128, 128], BF16)
    B.padA = sbt("padA", [128, 128], BF16)
    B.denP = sbt("denP", [128, 128], BF16)
    B.recD = sbt("recD", [128, 128], F32)
    B.tiny = sbt("tiny", [128, 32], F32)
    B.biasT = sbt("biasT", [128, 128], F32)
    B.dacc = sbt("dacc", [128, 144], F32)
    B.onehb = sbt("onehb", [128, 256], BF16)
    B.ropetmp = sbt("ropetmp", [128, 128], F32)
    B.maskA = sbt("maskA", [128, KB], F32)
    B.dbg = sbt("dbgbuf", [128, 1024], F32)
    B.den_scr = sbt("den_scrx", [128, 64], F32)
    B.cc = lambda j: B.cst[:, j:j + 1]

    B.S = {name: Sem(es, nc, name) for name in (
        "gl", "sl", "trA", "trB", "mmA", "vecA", "scaA", "ph")}
    B.psA = es.enter_context(nc.psum_tensor("psA", [128, 1024], F32))[:, :]
    B.psB = es.enter_context(nc.psum_tensor("psB", [128, 1024], F32))[:, :]
    B.psC = es.enter_context(nc.psum_tensor("psC", [128, 1024], F32))[:, :]
    B.psD = es.enter_context(nc.psum_tensor("psD", [128, 1024], F32))[:, :]
    return B


def tap(K, B, name, ap):
    if name in B.taps:
        ph, gl = B.S['ph'], B.S['gl']
        K.barrier(ph)
        K.dma(G, B.taps[name], ap, inc=gl)
        K.wait(G, gl, gl.c)
        K.barrier(ph)


def S_trB_warm(B):
    return B.S['trB']


def phase0(K, B):
    sl, vecA = B.S['sl'], B.S['vecA']
    cc = B.cc
    maskf = B.bufF(FR, KB)
    onehf = B.bufF(FR + 1024, 256)
    K.dma(SC, B.tiny[:, 0:16], B.cst_d, inc=S_trB_warm(B))
    K.wait(SC, B.S['sl'], B.S['sl'].c * 0 + 16)
    K.op(SC, lambda e: e.activation(B.tiny[0:1, 30:31], B.tiny[0:1, 0:1],
                                    AF.Sqrt), inc=B.S['scaA'])
    for dst, src in ((B.cst, B.cst_d), (B.qnw, B.qnw_d), (B.kvnw, B.kvnw_d),
                     (B.gsc, B.gsc_d), (B.usc, B.usc_d), (B.dsc, B.dsc_d),
                     (maskf, B.mask_d), (onehf, B.oneh_d)):
        K.dma(SY, dst, src, inc=sl)
    K.wait(VE, sl, sl.c)
    K.op(VE, lambda e: e.tensor_copy(B.onehb, onehf))
    for src, dst in ((B.gsc, B.gsc2), (B.usc, B.usc2), (B.dsc, B.dsc2)):
        K.op(VE, lambda e, src=src, dst=dst: e.tensor_scalar(
            dst, src, cc(C_M128), None, ALU.mult))
    K.op(VE, lambda e: e.tensor_copy(B.mask01, maskf))
    K.op(VE, lambda e: e.tensor_scalar(
        B.maskA, maskf, -1.0e30, 1.0e30, ALU.mult, ALU.add))
    K.op(VE, lambda e: e.memset(B.ones_row, 1.0))
    K.op(VE, lambda e: e.memset(B.ones_bfc, 1.0))
    for z in (B.stats, B.statB, B.statC, B.padA, B.denP, B.dacc):
        K.op(VE, lambda e, z=z: e.memset(z, 0.0))
    K.drain(VE)
    K.op(VE, lambda e: e.nop(), inc=vecA)


def phase1(K, B):
    S = B.S
    gl, sl, trA, trB = S['gl'], S['sl'], S['trA'], S['trB']
    mmA, vecA, scaA = S['mmA'], S['vecA'], S['scaA']
    cc, tiny, stats, statB = B.cc, B.tiny, B.stats, B.statB
    psA, psB, psC, psD = B.psA, B.psB, B.psC, B.psD

    wkva = B.bufB(FR + 0, 16 * CK)
    wkr = B.bufB(FR + 4096, 16 * DR)
    wkvb = B.bufB(FR + 4608, 4 * 4096)
    xkbf = B.bufB(FR + 12800, 2 * D)
    xkT = B.bufB(FR + 14848, 2 * D)
    ckvbf = B.bufB(FR + 16896, 2 * CK)
    ckvTs = B.bufB(FR + 17408, 4 * KB)
    krn = B.bufB(FR + 19456, 2 * 128)
    sqs = B.bufB(FR + 19584, CK)
    ckst = B.bufF(FR + 19840, 64)
    skst = B.bufF(FR + 19904, 64)
    krotT = B.bufB(A_KROT, T)

    for c in range(16):
        K.dma(G, wkva[:, c * CK:(c + 1) * CK],
              B.wkva_d[c * 128:(c + 1) * 128, :])
        K.dma(G, wkr[:, c * DR:(c + 1) * DR],
              B.wkr_d[c * 128:(c + 1) * 128, :],
              inc=gl if c == 15 else None)
    for c in range(4):
        K.dma(G, wkvb[:, c * 4096:(c + 1) * 4096],
              B.wkvb_d[c * 128:(c + 1) * 128, :],
              inc=gl if c == 3 else None)
    p1_w = gl.c
    K.wait(VE, gl, p1_w)
    for c in range(4):
        K.op(VE, lambda e, c=c: e.tensor_scalar(
            wkvb[:, c * 4096:(c + 1) * 4096],
            wkvb[:, c * 4096:(c + 1) * 4096],
            B.kvnw[:, c:c + 1], None, ALU.mult))
    K.op(VE, lambda e: e.memset(krn, 0.0), inc=vecA)
    p1_fold = vecA.c

    tr_d, ld_d, ck_d, mm_d = {}, {}, {}, {}
    st_d = {-2: 0, -1: 0}
    ev_d = {-2: 0, -1: 0}
    ctr_d = {-2: 0, -1: 0}
    sp_d = {-1: 0}
    for t in range(64):
        b, tin, pp = t // 8, t % 8, t % 2
        if t >= 2:
            K.wait(G, trA, tr_d[t - 2])
        K.dma(G, xkbf[:, pp * D:(pp + 1) * D],
              B.xk_d[t * 128:(t + 1) * 128, :], inc=gl)
        ld_d[t] = gl.c
        K.dma(SY, ckst[:, pp * 32:(pp + 1) * 32],
              B.ckt_d[t * 128:(t + 1) * 128, :])
        K.dma(SY, skst[:, pp * 32:(pp + 1) * 32],
              B.skt_d[t * 128:(t + 1) * 128, :], inc=sl)
        ck_d[t] = sl.c
        K.wait(SY, gl, ld_d[t])
        if t >= 2:
            K.wait(SY, mmA, mm_d[t - 2])
        for c in range(16):
            K.dma(SY, xkT[:, pp * D + c * 128: pp * D + (c + 1) * 128],
                  xkbf[:, pp * D + c * 128: pp * D + (c + 1) * 128],
                  inc=trA if c == 15 else None, transpose=True)
        tr_d[t] = trA.c
        ps = psA if pp == 0 else psB
        K.wait(TE, trA, tr_d[t])
        if t == 0:
            K.wait(TE, gl, p1_w)
            K.wait(TE, vecA, p1_fold)
        K.wait(TE, scaA, st_d[t - 2])
        K.wait(TE, vecA, ev_d[t - 2])
        for c in range(16):
            lhs = xkT[:, pp * D + c * 128: pp * D + (c + 1) * 128]
            K.mm(ps[:, 0:CK], lhs, wkva[:, c * CK:(c + 1) * CK],
                 start=(c == 0), stop=(c == 15))
        for c in range(16):
            lhs = xkT[:, pp * D + c * 128: pp * D + (c + 1) * 128]
            K.mm(ps[:, 512:576], lhs, wkr[:, c * DR:(c + 1) * DR],
                 start=(c == 0), stop=(c == 15),
                 inc=mmA if c == 15 else None)
        mm_d[t] = mmA.c
        K.wait(SC, mmA, mm_d[t])
        K.op(SC, lambda e, ps=ps, tin=tin: e.activation(
            sqs, ps[:, 0:CK], AF.Square, accum_out=stats[:, tin:tin + 1]))
        K.op(SC, lambda e, ps=ps, t=t: e.activation(
            sqs[:, 0:DR], ps[:, 512:576], AF.Square,
            accum_out=statB[:, t:t + 1]), inc=scaA)
        st_d[t] = scaA.c
        K.wait(VE, scaA, st_d[t])
        K.wait(VE, sl, ck_d[t])
        K.op(VE, lambda e, tin=tin: e.tensor_scalar(
            tiny[:, 0:1], stats[:, tin:tin + 1], cc(C_I512), cc(C_EPSR),
            ALU.mult, ALU.add), inc=vecA)
        K.wait(SC, vecA, vecA.c)
        K.op(SC, lambda e: e.activation(tiny[:, 1:2], tiny[:, 0:1], AF.Sqrt),
             inc=scaA)
        K.wait(VE, scaA, scaA.c)
        K.op(VE, lambda e: e.reciprocal(tiny[:, 2:3], tiny[:, 1:2]))
        K.drain(VE)
        if t >= 2:
            K.wait(VE, trB, ctr_d[t - 2])
        K.op(VE, lambda e, ps=ps, pp=pp: e.tensor_scalar(
            ckvbf[:, pp * CK:(pp + 1) * CK], ps[:, 0:CK],
            tiny[:, 2:3], None, ALU.mult))
        x1, x2_ = ps[:, 512:544], ps[:, 544:576]
        ct = ckst[:, pp * 32:(pp + 1) * 32]
        st_ = skst[:, pp * 32:(pp + 1) * 32]
        kn = krn[:, pp * 128: pp * 128 + 64]
        K.op(VE, lambda e, x1=x1, ct=ct: e.tensor_mul(stats[:, 64:96], x1, ct))
        K.op(VE, lambda e, x2_=x2_, st_=st_: e.tensor_mul(
            stats[:, 96:128], x2_, st_))
        K.op(VE, lambda e, x2_=x2_, ct=ct: e.tensor_mul(
            B.statB[:, 64:96], x2_, ct))
        K.op(VE, lambda e, x1=x1, st_=st_: e.tensor_mul(
            B.statB[:, 96:128], x1, st_))
        K.drain(VE)
        K.op(VE, lambda e, kn=kn: e.tensor_sub(
            kn[:, 0:32], stats[:, 64:96], stats[:, 96:128]))
        K.op(VE, lambda e, kn=kn: e.tensor_add(
            kn[:, 32:64], B.statB[:, 64:96], B.statB[:, 96:128]))
        K.drain(VE)
        K.op(VE, lambda e, pp=pp: e.tensor_copy(
            krn[:, pp * 128 + 64: pp * 128 + 128],
            krn[:, pp * 128: pp * 128 + 64]), inc=vecA)
        ev_d[t] = vecA.c
        K.wait(SY, vecA, ev_d[t])
        if tin == 0:
            K.wait(SY, gl, sp_d[b - 1])
        for c in range(4):
            K.dma(SY, ckvTs[:, c * KB + tin * 128: c * KB + (tin + 1) * 128],
                  ckvbf[:, pp * CK + c * 128: pp * CK + (c + 1) * 128],
                  transpose=True)
        K.dma(SY, krotT[:, t * 128:(t + 1) * 128],
              krn[:, pp * 128:(pp + 1) * 128], inc=trB, transpose=True)
        ctr_d[t] = trB.c
        if tin == 7:
            K.wait(G, trB, ctr_d[t])
            for c in range(4):
                K.dma(G, B.ckvT_d[:, c * T + b * KB: c * T + (b + 1) * KB],
                      ckvTs[:, c * KB:(c + 1) * KB],
                      inc=gl if c == 3 else None)
            sp_d[b] = gl.c
    K.wait(VE, scaA, scaA.c)
    K.op(VE, lambda e: e.reduce_max(tiny[:, 3:4], statB[:, 0:64], AX.X))
    K.drain(VE)
    K.op(VE, lambda e: e.tensor_copy(B.statC[:, 0:1], tiny[:, 3:4]),
         inc=vecA)
    K.wait(SY, vecA, vecA.c)
    K.dma(SY, B.padA, B.statC, inc=trB, transpose=True)
    K.wait(VE, trB, trB.c)
    K.op(VE, lambda e: e.reduce_max(tiny[:, 4:5], B.padA[:, 0:128],
                                    AX.X))
    K.op(VE, lambda e: e.tensor_copy(B.denP, B.statC), inc=vecA)
    K.wait(SC, vecA, vecA.c)
    K.op(SC, lambda e: e.activation(tiny[0:1, 5:6], tiny[0:1, 4:5], AF.Sqrt),
         inc=scaA)
    K.wait(TE, scaA, scaA.c)
    K.mm(psC[:, 0:1], B.ones_row[0:1, 0:128], tiny[0:1, 5:6],
         start=True, stop=True, inc=mmA)
    K.wait(VE, mmA, mmA.c)
    K.op(VE, lambda e: e.tensor_copy(tiny[:, 5:6], psC[:, 0:1]), inc=vecA)
    mk_done = vecA.c
    g_d = {-2: mk_done, -1: mk_done}
    for h in range(H):
        ps = psC if h % 2 == 0 else psD
        K.wait(TE, vecA, g_d[h - 2])
        for c in range(4):
            w = wkvb[:, c * 4096 + h * 256: c * 4096 + h * 256 + 128]
            K.mm(ps[:, 0:128], w, w, start=(c == 0), stop=(c == 3),
                 inc=mmA if c == 3 else None)
        K.wait(VE, mmA, mmA.c)
        K.op(VE, lambda e, h=h, ps=ps: e.reduce_sum(
            statB[:, h:h + 1], ps[:, 0:128], AX.X,
            apply_absolute_value=True), inc=vecA)
        g_d[h] = vecA.c
    K.drain(VE)
    K.op(VE, lambda e: e.tensor_copy(B.statC[:, 0:16], statB[:, 0:16]),
         inc=vecA)
    K.wait(SY, vecA, vecA.c)
    K.dma(SY, B.padA, B.statC, inc=trB, transpose=True)
    K.wait(VE, trB, trB.c)
    K.op(VE, lambda e: e.reduce_max(tiny[:, 7:8], B.padA[:, 0:128], AX.X),
         inc=vecA)
    K.wait(SC, vecA, vecA.c)
    K.op(SC, lambda e: e.activation(tiny[:, 8:9], tiny[:, 7:8], AF.Sqrt),
         inc=scaA)
    K.wait(VE, scaA, scaA.c)
    K.drain(VE)
    K.op(VE, lambda e: e.tensor_scalar(tiny[:, 9:10], tiny[:, 5:6],
                                       cc(C_SQK), None, ALU.mult))
    K.drain(VE)
    K.op(VE, lambda e: e.scalar_tensor_tensor(
        tiny[:, 10:11], tiny[:, 8:9], cc(C_B512), tiny[:, 9:10],
        ALU.mult, ALU.add))
    K.drain(VE)
    K.op(VE, lambda e: e.nop(), inc=vecA)


def phase2(K, B):
    S = B.S
    gl, sl, trA, trB = S['gl'], S['sl'], S['trA'], S['trB']
    mmA, vecA, scaA = S['mmA'], S['vecA'], S['scaA']
    cc, tiny, stats, statB = B.cc, B.tiny, B.stats, B.statB
    psA, psB, psC, psD = B.psA, B.psB, B.psC, B.psD

    qaT = B.bufB(FR + 0, 12 * R)
    wqa = B.bufB(FR + 6144, 16 * CQ)
    wqb = B.bufB(FR + 6144, 12 * 1536)
    xqbf = B.bufB(FR + 18432, 2 * D)
    xqT = B.bufB(FR + 20480, 2 * D)
    qabf = B.bufB(FR + 22528, CQ)
    qfn = B.bufB(FR + 23296, 1728)
    cqs = B.bufF(FR + 18432, 256)
    sqt = B.bufF(FR + 18688, 256)
    rpak = B.bufB(FR + 24160, 4 * 128)
    qfT = B.bufB(A_QFT, 24 * R)
    B.qfT = qfT

    for c in range(16):
        K.dma(G, wqa[:, c * CQ:(c + 1) * CQ],
              B.wqa_d[c * 128:(c + 1) * 128, :],
              inc=gl if c == 15 else None)
    p2_w = gl.c
    xtr_d, xl_d, qa_mm = {}, {}, {}
    qa_ev = {-2: 0, -1: 0}
    qa_vv = {}
    qatr = {}
    for s in range(NSUB):
        pp = s % 2
        if s >= 2:
            K.wait(G, trA, xtr_d[s - 2])
        K.dma(G, xqbf[:, pp * D:(pp + 1) * D],
              B.xq_d[s * 128:(s + 1) * 128, :], inc=gl)
        xl_d[s] = gl.c
        K.wait(SY, gl, xl_d[s])
        if s >= 2:
            K.wait(SY, mmA, qa_mm[s - 2])
        for c in range(16):
            K.dma(SY, xqT[:, pp * D + c * 128: pp * D + (c + 1) * 128],
                  xqbf[:, pp * D + c * 128: pp * D + (c + 1) * 128],
                  inc=trA if c == 15 else None, transpose=True)
        xtr_d[s] = trA.c
        p0 = psA if pp == 0 else psC
        p1_ = psB if pp == 0 else psD
        K.wait(TE, trA, xtr_d[s])
        if s == 0:
            K.wait(TE, gl, p2_w)
        K.wait(TE, scaA, qa_ev[s - 2])
        for n in range(3):
            dst = p0[:, n * 512:(n + 1) * 512] if n < 2 else p1_[:, 0:512]
            for c in range(16):
                K.mm(dst, xqT[:, pp * D + c * 128: pp * D + (c + 1) * 128],
                     wqa[:, c * CQ + n * 512: c * CQ + (n + 1) * 512],
                     start=(c == 0), stop=(c == 15),
                     inc=mmA if (c == 15 and n == 2) else None)
        qa_mm[s] = mmA.c
        tb = 12 + (s % 2) * 8
        K.wait(SC, mmA, qa_mm[s])
        K.wait(SC, vecA, qa_vv.get(s - 2, 0))
        K.op(SC, lambda e, p0=p0, tb=tb: e.activation(
            qfn[:, 0:KB], p0[:, 0:KB], AF.Square,
            accum_out=tiny[:, tb:tb + 1]))
        K.op(SC, lambda e, p1_=p1_, tb=tb: e.activation(
            qfn[:, 0:512], p1_[:, 0:512], AF.Square,
            accum_out=tiny[:, tb + 1:tb + 2]), inc=scaA)
        K.wait(VE, scaA, scaA.c)
        K.op(VE, lambda e, tb=tb: e.tensor_add(
            tiny[:, tb + 2:tb + 3], tiny[:, tb:tb + 1],
            tiny[:, tb + 1:tb + 2]))
        K.drain(VE)
        K.op(VE, lambda e, tb=tb: e.tensor_scalar(
            tiny[:, tb + 2:tb + 3], tiny[:, tb + 2:tb + 3], cc(C_I1536),
            cc(C_EPSR), ALU.mult, ALU.add), inc=vecA)
        K.wait(SC, vecA, vecA.c)
        K.op(SC, lambda e, tb=tb: e.activation(
            tiny[:, tb + 3:tb + 4], tiny[:, tb + 2:tb + 3], AF.Sqrt),
            inc=scaA)
        K.wait(VE, scaA, scaA.c)
        K.op(VE, lambda e, tb=tb: e.reciprocal(tiny[:, tb + 4:tb + 5],
                                               tiny[:, tb + 3:tb + 4]))
        K.drain(VE)
        K.wait(VE, trB, qatr.get(s - 1, 0))
        K.op(VE, lambda e, p0=p0, tb=tb: e.tensor_scalar(
            qabf[:, 0:KB], p0[:, 0:KB], tiny[:, tb + 4:tb + 5], None,
            ALU.mult))
        K.op(VE, lambda e, p1_=p1_, tb=tb: e.tensor_scalar(
            qabf[:, KB:CQ], p1_[:, 0:512], tiny[:, tb + 4:tb + 5], None,
            ALU.mult), inc=vecA)
        qa_evv = vecA.c
        qa_vv[s] = vecA.c
        K.wait(SC, vecA, qa_evv)
        K.op(SC, lambda e: e.nop(), inc=scaA)
        qa_ev[s] = scaA.c
        K.wait(SY, vecA, qa_evv)
        for c in range(12):
            K.dma(SY, qaT[:, c * R + s * 128: c * R + (s + 1) * 128],
                  qabf[:, c * 128:(c + 1) * 128],
                  inc=trB if c == 11 else None, transpose=True)
        qatr[s] = trB.c
    p2_qaT = trB.c

    q_ev = {}
    qtr_prev = 0
    qf_prev = [0]
    for hf in range(2):
        K.wait(G, mmA, mmA.c)
        for c in range(12):
            K.dma(G, wqb[:, c * 1536:(c + 1) * 1536],
                  B.wqb_d[c * 128:(c + 1) * 128, hf * 1536:(hf + 1) * 1536],
                  inc=gl if c == 11 else None)
        K.wait(VE, gl, gl.c)
        for c in range(12):
            K.op(VE, lambda e, c=c: e.tensor_scalar(
                wqb[:, c * 1536:(c + 1) * 1536],
                wqb[:, c * 1536:(c + 1) * 1536],
                B.qnw[:, c:c + 1], None, ALU.mult),
                inc=vecA if c == 11 else None)
        p2_fold = vecA.c
        for s in range(NSUB):
            pp = s % 2
            p0 = psA if pp == 0 else psC
            p1_ = psB if pp == 0 else psD
            if hf == 0 and s == 0:
                K.wait(TE, trB, p2_qaT)
            if s == 0:
                K.wait(TE, vecA, p2_fold)
            K.wait(TE, scaA, q_ev.get((hf, s - 2),
                                      q_ev.get((hf - 1, s + 6), 0)))
            for n in range(3):
                dst = p0[:, n * 512:(n + 1) * 512] if n < 2 else p1_[:, 0:512]
                for c in range(12):
                    K.mm(dst, qaT[:, c * R + s * 128: c * R + (s + 1) * 128],
                         wqb[:, c * 1536 + n * 512: c * 1536 + (n + 1) * 512],
                         start=(c == 0), stop=(c == 11),
                         inc=mmA if (c == 11 and n == 2) else None)
            q_mm = mmA.c
            K.wait(SY, vecA, qf_prev[0])
            K.dma(SY, cqs, B.cqr_d[s * 128:(s + 1) * 128,
                                   hf * 256:(hf + 1) * 256])
            K.dma(SY, sqt, B.sqr_d[s * 128:(s + 1) * 128,
                                   hf * 256:(hf + 1) * 256], inc=sl)
            rp_ld = sl.c
            K.wait(VE, mmA, q_mm)
            K.wait(VE, sl, rp_ld)
            K.wait(VE, trB, qtr_prev)
            K.op(VE, lambda e, p0=p0: e.tensor_copy(
                win(qfn, 0, 5, 192, 128), win(p0, 0, 5, 192, 128)))
            K.op(VE, lambda e, p0=p0: e.tensor_copy(
                qfn[:, 960:1024], p0[:, 960:1024]))
            K.op(VE, lambda e, p1_=p1_: e.tensor_copy(
                qfn[:, 1024:1088], p1_[:, 0:64]))
            K.op(VE, lambda e, p1_=p1_: e.tensor_copy(
                win(qfn, 6 * 192, 2, 192, 128), win(p1_, 128, 2, 192, 128)))

            def rope(pt, rel0, l0, nl):
                c3 = cqs.rearrange("p (l f) -> p l f", f=32)[:, l0:l0 + nl]
                s3 = sqt.rearrange("p (l f) -> p l f", f=32)[:, l0:l0 + nl]
                x1v = win(pt, rel0, nl, 192, 32)
                x2v = win(pt, rel0 + 32, nl, 192, 32)
                o1 = win(qfn, l0 * 192 + 128, nl, 192, 32)
                o2 = win(qfn, l0 * 192 + 160, nl, 192, 32)
                t1 = win(stats, 0, nl, 32, 32)
                t2 = win(statB, 0, nl, 32, 32)
                t3 = win(B.recD, 0, nl, 32, 32)
                t4 = win(B.ropetmp, 0, nl, 32, 32)
                K.op(VE, lambda e: e.tensor_mul(t1, x1v, c3))
                K.op(VE, lambda e: e.tensor_mul(t2, x2v, s3))
                K.op(VE, lambda e: e.tensor_mul(t3, x2v, c3))
                K.op(VE, lambda e: e.tensor_mul(t4, x1v, s3))
                K.drain(VE)
                K.op(VE, lambda e: e.tensor_sub(o1, t1, t2))
                K.op(VE, lambda e: e.tensor_add(o2, t3, t4))

            rope(p0, 128, 0, 4)
            rope(p0, 4 * 192 + 128, 4, 1)
            rope(p1_, 64, 5, 3)
            K.drain(VE)
            for j in range(4):
                K.op(VE, lambda e, j=j: e.tensor_copy(
                    rpak[:, j * 128:(j + 1) * 128].rearrange(
                        "p (a b) -> p a b", b=64),
                    win(qfn, (2 * j) * 192 + 128, 2, 192, 64)))
            K.op(VE, lambda e: e.nop(), inc=vecA)
            qf_ev = vecA.c
            qf_prev[0] = vecA.c
            K.wait(SC, vecA, qf_ev)
            K.op(SC, lambda e: e.nop(), inc=scaA)
            q_ev[(hf, s)] = scaA.c
            K.wait(SY, vecA, qf_ev)
            for l in range(8):
                hg = hf * 8 + l
                K.dma(SY, qfT[:, hg * R + s * 128: hg * R + (s + 1) * 128],
                      qfn[:, l * 192: l * 192 + 128], transpose=True)
            for j in range(4):
                rc = 16 + hf * 4 + j
                K.dma(SY, qfT[:, rc * R + s * 128: rc * R + (s + 1) * 128],
                      rpak[:, j * 128:(j + 1) * 128],
                      inc=trB if j == 3 else None, transpose=True)
            qtr_prev = trB.c
    B.p2_qfT_done = trB.c


def phase25(K, B):
    S = B.S
    sl, trB = S['sl'], S['trB']
    mmA, vecA, scaA = S['mmA'], S['vecA'], S['scaA']
    cc, tiny = B.cc, B.tiny
    psA, psB = B.psA, B.psB
    qfT = B.qfT
    trash = B.bufB(FR + 18432, KB)
    trash2 = B.bufB(FR + 19456, KB)
    sumq = B.bufF(FR + 0, KB)
    mbf = B.bufB(FR + 2048, KB)

    mm_h = {-1: 0}
    for h in range(H):
        bb = (h % 2) * 64
        rc = 16 + h // 2
        K.wait(SC, trB, B.p2_qfT_done)
        K.wait(SC, mmA, mm_h[h - 1])
        K.op(SC, lambda e, h=h: e.activation(
            trash, qfT[:, h * R:(h + 1) * R], AF.Square))
        K.op(SC, lambda e, rc=rc, bb=bb: e.activation(
            trash2[bb:bb + 64, :],
            qfT[bb:bb + 64, rc * R:(rc + 1) * R], AF.Square), inc=scaA)
        K.wait(TE, scaA, scaA.c)
        for nh in range(2):
            K.mm(psA[0:16, nh * 512:(nh + 1) * 512],
                 B.onehb[0:128, h * 16:(h + 1) * 16],
                 trash[:, nh * 512:(nh + 1) * 512],
                 start=(h == 0), stop=False)
            K.mm(psA[0:16, nh * 512:(nh + 1) * 512],
                 B.onehb[bb:bb + 64, h * 16:(h + 1) * 16],
                 trash2[bb:bb + 64, nh * 512:(nh + 1) * 512],
                 start=False, stop=(h == 15),
                 inc=mmA if nh == 1 else None)
        mm_h[h] = mmA.c
    K.wait(VE, mmA, mm_h[H - 1])
    K.op(VE, lambda e: e.tensor_copy(sumq[0:16, :], psA[0:16, 0:R]),
         inc=vecA)
    K.wait(SC, vecA, vecA.c)
    K.op(SC, lambda e: e.activation(sumq[0:16, :], sumq[0:16, :], AF.Sqrt),
         inc=scaA)
    K.wait(VE, scaA, scaA.c)
    K.op(VE, lambda e: e.tensor_scalar(
        sumq[0:16, :], sumq[0:16, :], tiny[0:16, 10:11], None, ALU.mult))
    K.drain(VE)
    K.op(VE, lambda e: e.tensor_scalar(
        sumq[0:16, :], sumq[0:16, :], cc(C_NEG1)[0:16, :],
        cc(C_SHIFT)[0:16, :], ALU.mult, ALU.add))
    K.drain(VE)
    K.op(VE, lambda e: e.tensor_copy(mbf[0:16, :], sumq[0:16, :]), inc=vecA)
    K.wait(SY, vecA, vecA.c)
    for j in range(8):
        K.dma(SY, B.denP[:, j * 16:(j + 1) * 16],
              mbf[0:16, j * 128:(j + 1) * 128],
              inc=trB if j == 7 else None, transpose=True)
    K.wait(VE, trB, trB.c)
    K.op(VE, lambda e: e.tensor_copy(B.biasT, B.denP))
    K.drain(VE)
    K.op(VE, lambda e: e.memset(B.denP, 0.0), inc=vecA)


def phase3(K, B):
    S = B.S
    gl, sl, trA, trB = S['gl'], S['sl'], S['trA'], S['trB']
    mmA, vecA, scaA = S['mmA'], S['vecA'], S['scaA']
    cc = B.cc
    psA, psB, psC, psD = B.psA, B.psB, B.psC, B.psD
    qfT = B.qfT
    krotT = B.bufB(A_KROT, T)
    kpT = B.bufB(FR + 8192, HPH * KB)
    vbf = B.bufB(FR + 12288, NSUB * KB)
    pT = B.bufB(FR + 16384, NSUB * KB)
    pscr = B.bufB(FR + 20480, 4 * KB)
    ckv = B.bufB(FR + 22528, 4 * KB)
    wkvbh = B.bufB(A_OTN, 4 * 2048)
    wkvbx = B.bufB(A_OTN, 4 * 2048 + 128)
    oTn = B.bufB(A_OTN, HPH * R)
    oTf = lambda hl, q0, w: B.bufF(FR + hl * 1024 + q0, w)

    K.op(SC, lambda e: e.activation(B.tiny[0:1, 30:31], B.cst[0:1, 8:9],
                                    AF.Exp), inc=B.S['scaA'])
    slot_exp = [0, 0]
    slot_tr = [0, 0, 0, 0]
    fold_prev = 0
    otn_half_done = 0
    for hf in range(2):
        K.wait(G, vecA, otn_half_done)
        for c in range(4):
            K.dma(G, wkvbh[:, c * 2048:(c + 1) * 2048],
                  B.wkvb_d[c * 128:(c + 1) * 128,
                           hf * 2048:(hf + 1) * 2048],
                  inc=gl if c == 3 else None)
        K.wait(VE, gl, gl.c)
        for c in range(4):
            K.op(VE, lambda e, c=c: e.tensor_scalar(
                wkvbh[:, c * 2048:(c + 1) * 2048],
                wkvbh[:, c * 2048:(c + 1) * 2048],
                B.kvnw[:, c:c + 1], None, ALU.mult),
                inc=vecA if c == 3 else None)
        wkvb_ok = vecA.c
        for s8 in range(8):
            K.op(VE, lambda e, s8=s8: e.memset(oTf(s8, 0, 1024), 0.0))
        K.op(VE, lambda e: e.nop(), inc=vecA)
        pv_prev = vecA.c

        kvmm_prev = 0
        pvmm_prev = 0
        for b in range(DBG_NB or NB):
            K.wait(G, mmA, kvmm_prev)
            for c in range(4):
                K.dma(G, ckv[:, c * KB:(c + 1) * KB],
                      B.ckvT_d[:, c * T + b * KB: c * T + (b + 1) * KB],
                      inc=gl if c == 3 else None)
            ckv_ld = gl.c
            ev0 = {-2: 0, -1: 0}
            for hl in range(HPH):
                ps = psC if hl % 2 == 0 else psD
                K.wait(TE, gl, ckv_ld)
                if hf == 0 and b == 0 and hl == 0:
                    K.wait(TE, vecA, wkvb_ok)
                if hl == 0:
                    K.wait(TE, vecA, pv_prev)
                K.wait(TE, scaA, ev0[hl - 2])
                for c in range(4):
                    w = wkvbh[:, c * 2048 + hl * 256:
                              c * 2048 + hl * 256 + 128]
                    for nh in range(2):
                        K.mm(ps[:, nh * 512:(nh + 1) * 512], w,
                             ckv[:, c * KB + nh * 512: c * KB + (nh + 1) * 512],
                             start=(c == 0), stop=(c == 3),
                             inc=mmA if (c == 3 and nh == 1) else None)
                K.wait(SC, mmA, mmA.c)
                K.op(SC, lambda e, hl=hl, ps=ps: e.activation(
                    kpT[:, hl * KB:(hl + 1) * KB], ps, AF.Copy), inc=scaA)
                ev0[hl] = scaA.c
            vev = {-2: ev0[HPH - 2], -1: ev0[HPH - 1]}
            for s8 in range(NSUB):
                ps = psC if s8 % 2 == 0 else psD
                K.wait(TE, scaA, vev[s8 - 2])
                for c in range(4):
                    lhs = ckv[:, c * KB + s8 * 128: c * KB + (s8 + 1) * 128]
                    rhs = win(wkvbx, c * 2048 + 128, HPH, 256, 128)
                    for nh in range(2):
                        K.mm(ps[:, nh * 512:(nh + 1) * 512], lhs,
                             rhs[:, nh * 4:(nh + 1) * 4, :],
                             start=(c == 0), stop=(c == 3),
                             inc=mmA if (c == 3 and nh == 1) else None)
                K.wait(SC, mmA, mmA.c)
                K.op(SC, lambda e, s8=s8, ps=ps: e.activation(
                    vbf[:, s8 * KB:(s8 + 1) * KB], ps, AF.Copy), inc=scaA)
                vev[s8] = scaA.c
            kv_ev = scaA.c
            kvmm_prev = mmA.c
            for hl in range(HPH):
                hg = hf * HPH + hl
                rc = 16 + hg // 2
                bb = (hg % 2) * 64
                sq = (NSUB - b) * 128
                inst_tr = {}
                K.wait(SC, mmA, pvmm_prev)
                for mi in range(b, NSUB):
                    inst = hl * (NSUB - b) + (mi - b)
                    ps = psA if inst % 2 == 0 else psB
                    K.wait(TE, scaA, kv_ev if inst == 0 else 0)
                    K.wait(TE, scaA, slot_exp[inst % 2])
                    for nh in range(2):
                        K.mm(ps[:, nh * 512:(nh + 1) * 512],
                             qfT[:, hg * R + mi * 128:
                                 hg * R + (mi + 1) * 128],
                             kpT[:, hl * KB + nh * 512:
                                 hl * KB + (nh + 1) * 512],
                             start=True, stop=False)
                        K.mm(ps[:, nh * 512:(nh + 1) * 512],
                             qfT[bb:bb + 64,
                                 rc * R + mi * 128: rc * R + (mi + 1) * 128],
                             krotT[bb:bb + 64,
                                   b * KB + nh * 512: b * KB + (nh + 1) * 512],
                             start=False, stop=True,
                             inc=mmA if nh == 1 else None)
                    sc_mm = mmA.c
                    slot = (hl * 8 + (mi - b)) % 4
                    if mi == b:
                        K.wait(VE, mmA, sc_mm)
                        K.op(VE, lambda e, ps=ps: e.tensor_sub(
                            ps, ps, B.maskA), inc=vecA)
                        K.wait(SC, vecA, vecA.c)
                    K.wait(SC, mmA, sc_mm)
                    K.wait(SC, trB, slot_tr[slot])
                    if inst == 0:
                        K.wait(SC, vecA, fold_prev)
                    K.op(SC, lambda e, slot=slot, ps=ps, mi=mi, hg=hg, hl=hl:
                         e.activation(
                             pscr[:, slot * KB:(slot + 1) * KB], ps, AF.Exp,
                             bias=B.biasT[:, mi * 16 + hg: mi * 16 + hg + 1],
                             scale=cc(C_SQK),
                             accum_out=B.den_scr[:, mi * 8 + hl:
                                                 mi * 8 + hl + 1]),
                         inc=scaA)
                    slot_exp[inst % 2] = scaA.c
                    for ks in range(8):
                        K.dma(SC, pT[:, ks * KB + (mi - b) * 128:
                                     ks * KB + (mi - b + 1) * 128],
                              pscr[:, slot * KB + ks * 128:
                                   slot * KB + (ks + 1) * 128],
                              inc=trB if ks == 7 else None, transpose=True)
                    slot_tr[slot] = trB.c
                    inst_tr[mi] = trB.c
                K.wait(TE, trB, inst_tr[NSUB - 1])
                K.wait(TE, vecA, pv_prev)
                for nq in range((sq + 511) // 512):
                    w_ = min(512, sq - nq * 512)
                    for ks in range(8):
                        K.mm(psC[:, nq * 512: nq * 512 + w_],
                             vbf[:, ks * KB + hl * 128:
                                 ks * KB + (hl + 1) * 128],
                             pT[:, ks * KB + nq * 512:
                                ks * KB + nq * 512 + w_],
                             start=(ks == 0), stop=(ks == 7),
                             inc=mmA if ks == 7 else None)
                pvmm_prev = mmA.c
                K.wait(VE, mmA, mmA.c)
                K.drain(VE)
                K.op(VE, lambda e, hl=hl, b=b, sq=sq: e.tensor_add(
                    oTf(hl, b * 128, sq), oTf(hl, b * 128, sq),
                    psC[:, 0:sq]))
                K.drain(VE)
                K.op(VE, lambda e: e.nop(), inc=vecA)
                pv_prev = vecA.c
            K.wait(VE, scaA, scaA.c)
            K.drain(VE)
            src = win(B.den_scr, b * 8, NSUB - b, 8, 8)
            dst = win(B.dacc, b * 16 + hf * 8, NSUB - b, 16, 8)
            K.op(VE, lambda e, src=src, dst=dst: e.tensor_add(dst, dst, src))
            K.drain(VE)
            K.op(VE, lambda e: e.nop(), inc=vecA)
            fold_prev = vecA.c
        obf = B.bufB(FR + 20480, KB)
        onat = B.bufB(FR + 20992, KB)
        onat2 = B.bufB(FR + 21504, KB)
        K.op(VE, lambda e, hf=hf: e.reciprocal(
            win(B.recD, 0, 8, 8, 8), win(B.dacc, hf * 8, 8, 16, 8)))
        K.drain(VE)
        K.op(VE, lambda e: e.nop(), inc=vecA)
        rec_ok = vecA.c
        tr1 = {-1: 0}
        ts_d = {-1: 0}
        tr2 = {-1: 0}
        for hl in range(HPH):
            K.wait(SC, trB, tr1[hl - 1])
            if hl == 0:
                K.wait(SC, vecA, fold_prev)
            K.op(SC, lambda e, hl=hl: e.activation(
                obf, oTf(hl, 0, KB), AF.Copy), inc=scaA)
            K.wait(SY, scaA, scaA.c)
            K.wait(SY, vecA, ts_d[hl - 1])
            for mi in range(NSUB):
                K.dma(SY, onat[:, mi * 128:(mi + 1) * 128],
                      obf[:, mi * 128:(mi + 1) * 128],
                      inc=trB if mi == 7 else None, transpose=True)
            tr1[hl] = trB.c
            K.wait(VE, trB, tr1[hl])
            if hl == 0:
                K.wait(VE, vecA, rec_ok)
            K.wait(VE, trB, tr2[hl - 1])
            for mi in range(NSUB):
                K.op(VE, lambda e, hl=hl, mi=mi: e.tensor_scalar(
                    onat2[:, mi * 128:(mi + 1) * 128],
                    onat[:, mi * 128:(mi + 1) * 128],
                    B.recD[:, mi * 8 + hl: mi * 8 + hl + 1], None,
                    ALU.mult))
            K.op(VE, lambda e: e.nop(), inc=vecA)
            ts_d[hl] = vecA.c
            K.wait(SY, vecA, ts_d[hl])
            for mi in range(NSUB):
                K.dma(SY, oTn[:, hl * R + mi * 128: hl * R + (mi + 1) * 128],
                      onat2[:, mi * 128:(mi + 1) * 128],
                      inc=trB if mi == 7 else None, transpose=True)
            tr2[hl] = trB.c
        K.wait(VE, trB, tr2[HPH - 1])
        K.op(VE, lambda e: e.nop(), inc=vecA)
        otn_half_done = vecA.c
        if hf == 0:
            K.wait(G, vecA, otn_half_done)
            K.dma(G, B.otn0_d, oTn, inc=gl)
            K.wait(G, gl, gl.c)
            tap(K, B, 'oTn0s', oTn)
            tap(K, B, 'recD0', B.recD)
            tap(K, B, 'dacc0', B.dacc)


def phase4(K, B):
    S = B.S
    gl, sl, trA, trB = S['gl'], S['sl'], S['trA'], S['trB']
    mmA, vecA, scaA = S['mmA'], S['vecA'], S['scaA']
    cc, stats, statB = B.cc, B.stats, B.statB
    psA, psB = B.psA, B.psB
    K.op(SC, lambda e: e.activation(B.tiny[0:1, 30:31], B.cst[0:1, 8:9],
                                    AF.Sqrt), inc=scaA)
    oTn1 = B.bufB(A_OTN, HPH * R)
    oTn0 = B.bufB(FR + 0, HPH * R)
    woh = B.bufB(FR + 4096, 16 * KB)
    xq = B.bufF(0, 16384)
    x2bf = B.bufB(32768 + 0, D)
    trash = B.bufB(33792, D)
    x2T = B.bufB(36864, 16 * R)
    B.x2T = x2T

    K.dma(G, oTn0, B.otn0_d, inc=gl)
    for s8 in range(NSUB):
        K.dma(SY, xq[:, s8 * D:(s8 + 1) * D],
              B.xq_d[s8 * 128:(s8 + 1) * 128, :],
              inc=sl if s8 == 7 else None)
    xq_ld = sl.c
    oTn0_ld = gl.c
    ev_d = {-2: 0, -1: 0}
    wo_mm_prev = 0
    for half in range(2):
        K.wait(G, mmA, wo_mm_prev)
        for c in range(16):
            K.dma(G, woh[:, c * KB:(c + 1) * KB],
                  B.wo_d[c * 128:(c + 1) * 128,
                         half * KB:(half + 1) * KB],
                  inc=gl if c == 15 else None)
        wo_ld = gl.c
        for s8 in range(NSUB):
            for db in range(2):
                k_ = half * 16 + s8 * 2 + db
                ps = psA if k_ % 2 == 0 else psB
                if s8 == 0 and db == 0:
                    K.wait(TE, gl, max(wo_ld, oTn0_ld))
                K.wait(TE, vecA, ev_d[k_ - 2])
                for hg in range(16):
                    src = oTn0 if hg < 8 else oTn1
                    K.mm(ps[:, 0:512],
                         src[:, (hg % 8) * R + s8 * 128:
                             (hg % 8) * R + (s8 + 1) * 128],
                         woh[:, hg * KB + db * 512: hg * KB + (db + 1) * 512],
                         start=(hg == 0), stop=(hg == 15),
                         inc=mmA if hg == 15 else None)
                K.wait(VE, mmA, mmA.c)
                if k_ == 0:
                    K.wait(VE, sl, xq_ld)
                sl_ = xq[:, s8 * D + half * KB + db * 512:
                         s8 * D + half * KB + (db + 1) * 512]
                K.op(VE, lambda e, ps=ps, sl_=sl_: e.tensor_add(
                    sl_, ps[:, 0:512], sl_), inc=vecA)
                ev_d[k_] = vecA.c
        wo_mm_prev = mmA.c
    K.drain(VE)
    y_done = vecA.c
    for s8 in range(NSUB):
        K.wait(VE, vecA, y_done)
        K.op(VE, lambda e, s8=s8: e.reduce_sum(
            stats[:, 32 + s8:33 + s8], xq[:, s8 * D:(s8 + 1) * D], AX.X),
            inc=vecA)
        K.wait(SC, vecA, vecA.c)
        K.op(SC, lambda e, s8=s8: e.activation(
            trash, xq[:, s8 * D:(s8 + 1) * D], AF.Square,
            accum_out=statB[:, 32 + s8:33 + s8]), inc=scaA)
    K.wait(VE, scaA, scaA.c)
    ln_chain(K, B, 32)
    ap_d = {}
    x2tr = {}
    for s8 in range(NSUB):
        K.op(VE, lambda e, s8=s8: e.tensor_scalar(
            xq[:, s8 * D:(s8 + 1) * D], xq[:, s8 * D:(s8 + 1) * D],
            stats[:, 40 + s8:41 + s8], statB[:, 40 + s8:41 + s8],
            ALU.subtract, ALU.mult), inc=vecA)
        ap_d[s8] = vecA.c
        K.wait(G, vecA, ap_d[s8])
        K.dma(G, B.x2_d[:, s8 * D:(s8 + 1) * D], xq[:, s8 * D:(s8 + 1) * D],
              inc=gl)
        K.wait(SC, vecA, ap_d[s8])
        K.wait(SC, trB, x2tr.get(s8 - 1, 0))
        K.op(SC, lambda e, s8=s8: e.activation(
            x2bf, xq[:, s8 * D:(s8 + 1) * D], AF.Copy), inc=scaA)
        K.wait(SY, scaA, scaA.c)
        for c in range(16):
            K.dma(SY, x2T[:, c * R + s8 * 128: c * R + (s8 + 1) * 128],
                  x2bf[:, c * 128:(c + 1) * 128],
                  inc=trB if c == 15 else None, transpose=True)
        x2tr[s8] = trB.c
    B.p4_x2T = trB.c
    B.p4_x2sp = gl.c


def ln_chain(K, B, base):
    cc, stats, statB = B.cc, B.stats, B.statB
    vecA, scaA = B.S['vecA'], B.S['scaA']
    s_ = slice(base, base + 8)
    d_ = slice(base + 8, base + 16)
    K.drain(VE)
    K.op(VE, lambda e: e.tensor_scalar(
        stats[:, d_], stats[:, s_], cc(C_ID), None, ALU.mult))
    K.op(VE, lambda e: e.tensor_scalar(
        statB[:, s_], statB[:, s_], cc(C_ID), None, ALU.mult))
    K.drain(VE)
    K.op(VE, lambda e: e.tensor_mul(B.tiny[:, 24:32], stats[:, d_],
                                    stats[:, d_]))
    K.drain(VE)
    K.op(VE, lambda e: e.tensor_sub(statB[:, s_], statB[:, s_],
                                    B.tiny[:, 24:32]))
    K.drain(VE)
    K.op(VE, lambda e: e.tensor_scalar(
        statB[:, s_], statB[:, s_], cc(C_ONE), cc(C_EPSLN),
        ALU.mult, ALU.add), inc=vecA)
    K.wait(SC, vecA, vecA.c)
    K.op(SC, lambda e: e.activation(statB[:, d_], statB[:, s_], AF.Sqrt),
         inc=scaA)
    K.wait(VE, scaA, scaA.c)
    K.op(VE, lambda e: e.reciprocal(statB[:, d_], statB[:, d_]))
    K.drain(VE)
    K.op(VE, lambda e: e.nop(), inc=vecA)


def phase5(K, B):
    S = B.S
    gl, sl, trA, trB = S['gl'], S['sl'], S['trA'], S['trB']
    mmA, vecA, scaA = S['mmA'], S['vecA'], S['scaA']
    psA, psB, psC = B.psA, B.psB, B.psC
    K.op(SC, lambda e: e.activation(B.tiny[0:1, 30:31], B.cst[0:1, 8:9],
                                    AF.Silu), inc=scaA)
    x2T = B.x2T
    WgT = B.bufB(0, 16 * KB)
    WuT = B.bufB(8192, 16 * KB)
    WdT = B.bufB(16384, 8 * D)
    raw = B.bufB(24576, 2 * D)
    deq = B.bufB(26624, D)
    hT = B.bufB(27648, NSUB * KB)
    sg = B.bufB(31744, 512)
    hnat = B.bufB(32000, 512)
    part = B.bufF(32256, 1024)

    hn_prevs = [0]
    deq_done = {-2: 0, -1: 0}
    tr_done = {-1: 0}
    gu_ev = {-2: 0, -1: 0}
    dn_ev = {-2: 0, -1: 0}
    dma_acc = {-2: 0, -1: 0}
    li = 0
    mm_macro_prev = 0
    for m in range(NSUB):
        wt0 = li
        K.wait(SY, mmA, mm_macro_prev)
        for wi in range(2):
            src_d = B.gw_d if wi == 0 else B.uw_d
            wdst = WgT if wi == 0 else WuT
            sc1 = B.gsc if wi == 0 else B.usc
            sc2 = B.gsc2 if wi == 0 else B.usc2
            for tt in range(8):
                it = m * 8 + tt
                pp = li % 2
                K.wait(G, vecA, deq_done[li - 2])
                K.dma(G, raw[:, pp * D:(pp + 1) * D],
                      src_d[it * 128:(it + 1) * 128, :], inc=gl)
                K.wait(VE, gl, gl.c)
                K.wait(VE, trB, tr_done[li - 1])
                K.op(VE, lambda e, pp=pp, it=it, sc1=sc1, sc2=sc2:
                     e.tensor_scalar(deq, raw[:, pp * D:(pp + 1) * D],
                                     sc1[:, it:it + 1], sc2[:, it:it + 1],
                                     ALU.mult, ALU.add), inc=vecA)
                deq_done[li] = vecA.c
                K.wait(SY, vecA, deq_done[li])
                for c in range(16):
                    K.dma(SY, wdst[:, c * KB + tt * 128:
                                   c * KB + (tt + 1) * 128],
                          deq[:, c * 128:(c + 1) * 128],
                          inc=trB if c == 15 else None, transpose=True)
                tr_done[li] = trB.c
                li += 1
        for ht in range(16):
            pp = li % 2
            K.wait(G, vecA, deq_done[li - 2])
            K.dma(G, raw[:, pp * D: pp * D + KB],
                  B.dw_d[ht * 128:(ht + 1) * 128, m * KB:(m + 1) * KB],
                  inc=gl)
            K.wait(VE, gl, gl.c)
            K.wait(VE, trB, tr_done[li - 1])
            K.op(VE, lambda e, pp=pp, ht=ht: e.tensor_scalar(
                deq[:, 0:KB], raw[:, pp * D: pp * D + KB],
                B.dsc[:, ht:ht + 1], B.dsc2[:, ht:ht + 1],
                ALU.mult, ALU.add), inc=vecA)
            deq_done[li] = vecA.c
            K.wait(SY, vecA, deq_done[li])
            for c in range(8):
                K.dma(SY, WdT[:, c * D + ht * 128: c * D + (ht + 1) * 128],
                      deq[:, c * 128:(c + 1) * 128],
                      inc=trB if c == 7 else None, transpose=True)
            tr_done[li] = trB.c
            li += 1
        w_done = trB.c
        htr = 0
        hn_prev = hn_prevs
        for s8 in range(NSUB):
            for ib in range(2):
                kk = m * 16 + s8 * 2 + ib
                ps = psA if kk % 2 == 0 else psB
                if s8 == 0 and ib == 0:
                    K.wait(TE, trB, w_done)
                    if m == 0:
                        K.wait(TE, trB, B.p4_x2T)
                K.wait(TE, scaA, gu_ev[kk - 2])
                for c in range(16):
                    K.mm(ps[:, 0:512],
                         x2T[:, c * R + s8 * 128: c * R + (s8 + 1) * 128],
                         WgT[:, c * KB + ib * 512: c * KB + (ib + 1) * 512],
                         start=(c == 0), stop=(c == 15))
                for c in range(16):
                    K.mm(ps[:, 512:1024],
                         x2T[:, c * R + s8 * 128: c * R + (s8 + 1) * 128],
                         WuT[:, c * KB + ib * 512: c * KB + (ib + 1) * 512],
                         start=(c == 0), stop=(c == 15),
                         inc=mmA if c == 15 else None)
                K.wait(SC, mmA, mmA.c)
                K.wait(SC, vecA, hn_prev[0])
                K.op(SC, lambda e, ps=ps: e.activation(
                    sg, ps[:, 0:512], AF.Silu), inc=scaA)
                K.wait(VE, scaA, scaA.c)
                if kk > 0:
                    K.wait(VE, trB, htr)
                K.op(VE, lambda e, ps=ps: e.tensor_mul(
                    hnat, ps[:, 512:1024], sg), inc=vecA)
                hn_ok = vecA.c
                hn_prev[0] = vecA.c
                K.wait(SC, vecA, hn_ok)
                K.op(SC, lambda e: e.nop(), inc=scaA)
                gu_ev[kk] = scaA.c
                K.wait(SY, vecA, hn_ok)
                for c2 in range(4):
                    ic = ib * 4 + c2
                    K.dma(SY, hT[:, ic * KB + s8 * 128:
                                 ic * KB + (s8 + 1) * 128],
                          hnat[:, c2 * 128:(c2 + 1) * 128],
                          inc=trB if c2 == 3 else None, transpose=True)
                htr = trB.c
        for s8 in range(NSUB):
            for hb in range(4):
                kk = m * 32 + s8 * 4 + hb
                pc = psC[:, (kk % 2) * 512:(kk % 2) * 512 + 512]
                if s8 == 0 and hb == 0:
                    K.wait(TE, trB, htr)
                K.wait(TE, vecA, dn_ev[kk - 2])
                for ic in range(8):
                    K.mm(pc, hT[:, ic * KB + s8 * 128:
                                ic * KB + (s8 + 1) * 128],
                         WdT[:, ic * D + hb * 512: ic * D + (hb + 1) * 512],
                         start=(ic == 0), stop=(ic == 7),
                         inc=mmA if ic == 7 else None)
                K.wait(VE, mmA, mmA.c)
                K.wait(VE, gl, dma_acc[kk - 2])
                pslot = part[:, (kk % 2) * 512:(kk % 2) * 512 + 512]
                K.op(VE, lambda e, pc=pc, pslot=pslot: e.tensor_copy(
                    pslot, pc), inc=vecA)
                dn_ev[kk] = vecA.c
                K.wait(G, vecA, dn_ev[kk])
                K.dma(G, B.yffn_d[:, s8 * D + hb * 512:
                                  s8 * D + (hb + 1) * 512],
                      pslot, inc=gl, accum=(m > 0))
                dma_acc[kk] = gl.c
        mm_macro_prev = mmA.c
    B.p5_done = gl.c


def phase6(K, B):
    S = B.S
    gl, sl = S['gl'], S['sl']
    mmA, vecA, scaA = S['mmA'], S['vecA'], S['scaA']
    K.op(SC, lambda e: e.activation(B.tiny[0:1, 30:31], B.cst[0:1, 8:9],
                                    AF.Sqrt), inc=scaA)
    cc, stats, statB = B.cc, B.stats, B.statB
    x2 = B.bufF(0, 16384)
    yf = B.bufF(16384, 16384)
    trash = B.bufB(33792, D)
    K.wait(SY, gl, B.p5_done)
    for s8 in range(NSUB):
        K.dma(SY, x2[:, s8 * D:(s8 + 1) * D], B.x2_d[:, s8 * D:(s8 + 1) * D])
        K.dma(SY, yf[:, s8 * D:(s8 + 1) * D],
              B.yffn_d[:, s8 * D:(s8 + 1) * D],
              inc=sl if s8 == 7 else None)
    ld = sl.c
    K.wait(VE, sl, ld)
    for s8 in range(NSUB):
        K.op(VE, lambda e, s8=s8: e.tensor_add(
            x2[:, s8 * D:(s8 + 1) * D], x2[:, s8 * D:(s8 + 1) * D],
            yf[:, s8 * D:(s8 + 1) * D]))
    K.drain(VE)
    for s8 in range(NSUB):
        K.op(VE, lambda e, s8=s8: e.reduce_sum(
            stats[:, 64 + s8:65 + s8], x2[:, s8 * D:(s8 + 1) * D], AX.X),
            inc=vecA)
        K.wait(SC, vecA, vecA.c)
        K.op(SC, lambda e, s8=s8: e.activation(
            trash, x2[:, s8 * D:(s8 + 1) * D], AF.Square,
            accum_out=statB[:, 64 + s8:65 + s8]), inc=scaA)
    K.wait(VE, scaA, scaA.c)
    ln_chain(K, B, 64)
    for s8 in range(NSUB):
        K.op(VE, lambda e, s8=s8: e.tensor_scalar(
            x2[:, s8 * D:(s8 + 1) * D], x2[:, s8 * D:(s8 + 1) * D],
            stats[:, 72 + s8:73 + s8], statB[:, 72 + s8:73 + s8],
            ALU.subtract, ALU.mult), inc=vecA)
        K.wait(SY, vecA, vecA.c)
        K.dma(SY, B.out_d[s8 * 128:(s8 + 1) * 128, :],
              x2[:, s8 * D:(s8 + 1) * D], inc=sl)
    K.wait(SY, sl, sl.c)


def program(K, B):
    for s_ in B.S.values():
        s_.c = 0
    K.dsem = {'gpsimd': B.S['gl'], 'sync': B.S['sl'], 'scalar': B.S['trB'],
              'vector': B.S['sl'], 'tensor': B.S['sl']}
    ph = B.S['ph']
    phase0(K, B)
    K.barrier(ph)
    if PHASE_LIMIT < 1:
        return
    phase1(K, B)
    K.barrier(ph)
    tap(K, B, 'krotT', B.bufB(A_KROT, T))
    tap(K, B, 'tiny', B.tiny)
    tap(K, B, 'ckvT', B.ckvT_d)
    tap(K, B, 'recD', B.recD)
    tap(K, B, 'denP', B.denP)
    if PHASE_LIMIT < 2:
        return
    phase2(K, B)
    K.barrier(ph)
    tap(K, B, 'qfT', B.qfT)
    tap(K, B, 'qaT', B.bufB(FR + 0, 12 * R))
    phase25(K, B)
    K.barrier(ph)
    tap(K, B, 'biasT', B.biasT)
    if PHASE_LIMIT < 3:
        return
    phase3(K, B)
    K.barrier(ph)
    tap(K, B, 'dacc', B.dacc)
    tap(K, B, 'kpT', B.bufB(FR + 8192, HPH * KB))
    tap(K, B, 'pT', B.bufB(FR + 16384, NSUB * KB))
    tap(K, B, 'maskA', B.maskA)
    tap(K, B, 'mask01t', B.mask01)
    tap(K, B, 'dbg', B.dbg)
    tap(K, B, 'vbf', B.bufB(FR + 12288, NSUB * KB))
    tap(K, B, 'biasT2', B.biasT)
    tap(K, B, 'oTn1', B.bufB(A_OTN, HPH * R))
    tap(K, B, 'otn0', B.otn0_d)
    if PHASE_LIMIT < 4:
        return
    phase4(K, B)
    K.barrier(ph)
    tap(K, B, 'x2T', B.x2T)
    tap(K, B, 'x2s', B.x2_d)
    if PHASE_LIMIT < 5:
        return
    phase5(K, B)
    K.barrier(ph)
    tap(K, B, 'yffn', B.yffn_d)
    if PHASE_LIMIT < 6:
        return
    phase6(K, B)


def build_nc(debug_taps=()):
    B = build(debug_taps)
    nc = B.nc
    with nc.Block() as block:
        for eng in ENGS:
            getattr(block, eng)(lambda _e, eng=eng: program(Ctx(nc, eng), B))
    return B


def _stage(inputs, core):
    x = np.asarray(inputs["L0_attn_hidden_states"])[0]
    idx = np.concatenate([np.arange(g * 1024 + core * 128,
                                    g * 1024 + core * 128 + 128)
                          for g in range(8)])
    d = {}
    d["xq"] = np.ascontiguousarray(x[idx])
    d["xk"] = np.ascontiguousarray(x)
    d["wqa"] = np.ascontiguousarray(np.asarray(inputs["L0_attn_Wq_a"]))
    d["wqb"] = np.ascontiguousarray(np.asarray(inputs["L0_attn_Wq_b"]))
    d["qnw"] = np.ascontiguousarray(
        np.asarray(inputs["L0_attn_q_norm_w"]).reshape(12, 128).T)
    d["wkva"] = np.ascontiguousarray(np.asarray(inputs["L0_attn_Wkv_a"]))
    d["wkvb"] = np.ascontiguousarray(np.asarray(inputs["L0_attn_Wkv_b"]))
    d["kvnw"] = np.ascontiguousarray(
        np.asarray(inputs["L0_attn_kv_norm_w"]).reshape(4, 128).T)
    d["wkr"] = np.ascontiguousarray(np.asarray(inputs["L0_attn_Wk_rope"]))
    d["wo"] = np.ascontiguousarray(np.asarray(inputs["L0_attn_Wo"]))
    d["gw"] = np.ascontiguousarray(np.asarray(inputs["L0_ffn_gate_w"],
                                              dtype=np.int32))
    d["uw"] = np.ascontiguousarray(np.asarray(inputs["L0_ffn_up_w"],
                                              dtype=np.int32))
    d["dw"] = np.ascontiguousarray(np.asarray(inputs["L0_ffn_down_w"],
                                              dtype=np.int32))
    d["gsc"] = np.ascontiguousarray(
        np.asarray(inputs["L0_ffn_gate_scale"]).reshape(64, 128).T)
    d["usc"] = np.ascontiguousarray(
        np.asarray(inputs["L0_ffn_up_scale"]).reshape(64, 128).T)
    d["dsc"] = np.ascontiguousarray(
        np.asarray(inputs["L0_ffn_down_scale"]).reshape(16, 128).T)
    inv = (1.0 / 10000.0 ** (np.arange(0, DR, 2, dtype=np.float32) / DR))
    ang = np.outer(np.arange(T, dtype=np.float32), inv)
    ckt, skt = np.cos(ang).astype(np.float32), np.sin(ang).astype(np.float32)
    d["ckt"], d["skt"] = ckt, skt
    d["cqr"] = np.ascontiguousarray(np.tile(ckt[idx], (1, H)))
    d["sqr"] = np.ascontiguousarray(np.tile(skt[idx], (1, H)))
    qi = 128 * core + np.arange(128)[:, None]
    kk = np.arange(KB)[None, :]
    d["mask01"] = (qi >= kk).astype(np.float32)
    cvals = np.zeros(16, np.float32)
    cvals[C_SQK] = SCALE_QK
    cvals[C_EPSR] = 1e-6
    cvals[C_I512] = 1.0 / 512.0
    cvals[C_EPSLN] = 1e-5
    cvals[C_ID] = 1.0 / 2048.0
    cvals[C_B512] = math.sqrt(512.0) * 1.05 * SCALE_QK
    cvals[C_I1536] = 1.0 / 1536.0
    cvals[C_M128] = -128.0
    cvals[C_ONE] = 1.0
    cvals[C_NEG1] = -1.0
    cvals[C_SHIFT] = EXP_SHIFT
    d["cst"] = np.tile(cvals[None, :], (128, 1))
    oneh = np.zeros((128, 256), np.float32)
    for h_ in range(16):
        oneh[:, h_ * 16 + h_] = 1.0
    d["oneh"] = oneh
    return d


_BUILD_CACHE = {}


def kernel(**inputs):
    key = tuple(sorted(DEBUG_TAPS)) + (PHASE_LIMIT,)
    if key not in _BUILD_CACHE:
        _BUILD_CACHE[key] = build_nc(
            tuple((n, s, d) for (n, s, d) in DEBUG_TAPS))
    B = _BUILD_CACHE[key]
    in_maps = [_stage(inputs, core) for core in range(8)]
    res = run_bass_kernel_spmd(B.nc, in_maps, core_ids=list(range(8)))
    out = np.zeros((T, D), np.float32)
    for core in range(8):
        oc = np.asarray(res.results[core]["out"])
        for g in range(8):
            out[g * 1024 + core * 128: g * 1024 + core * 128 + 128] = \
                oc[g * 128:(g + 1) * 128]
    if DEBUG_TAPS:
        kernel._tap_results = res.results
    return out.reshape(1, T, D)


# version 03
# speedup vs baseline: 1.9029x; 1.9029x over previous
import math
import numpy as np
import ml_dtypes
from contextlib import ExitStack

import concourse.bass as bass
import concourse.mybir as mybir
from concourse.bass_utils import run_bass_kernel_spmd

F32 = mybir.dt.float32
BF16 = mybir.dt.bfloat16
I32 = mybir.dt.int32
AF = mybir.ActivationFunctionType
ALU = mybir.AluOpType
AX = mybir.AxisListType
BF = ml_dtypes.bfloat16

T, D = 8192, 2048
R, NSUB, KB, NB = 1024, 8, 1024, 8
H, DN, DR, DQK, DV = 16, 128, 64, 192, 128
CQ, CK, DFF, HPH = 1536, 512, 8192, 8
SCALE_QK = 1.0 / math.sqrt(DQK)
EXP_SHIFT = 60.0

ENGS = ('gpsimd', 'sync', 'tensor', 'vector', 'scalar')
DBG_ONLY_ADD = None
DBG_NB = None
G, SY, TE, VE, SC = ENGS
PHASE_LIMIT = 99
DEBUG_TAPS = ()

(C_SQK, C_EPSR, C_I512, C_EPSLN, C_ID, C_B512, C_I1536, C_M128, C_ONE,
 C_NEG1, C_SHIFT) = range(11)

AR = 45056
A_KROT = 0
A_QFT = 4096
A_OTN = 16384
FR = 20480


def win(ap, start, n, stride, width):
    if n == 1:
        return ap[:, start:start + width].rearrange("p (a b) -> p a b", a=1)
    sub = ap[:, start:start + n * stride]
    return sub.rearrange("p (a b) -> p a b", b=stride)[:, :, 0:width]


def wtr(ap, start, n, stride, width):
    if stride == width:
        return ap[:, start:start + n * width].rearrange(
            "p (c q) -> p c q", q=width)
    lo = (start // stride) * stride
    off = start - lo
    assert off + width <= stride
    sub = ap[:, lo: lo + n * stride]
    return sub.rearrange("p (c q) -> p c q", q=stride)[:, :, off:off + width]


class Sem:
    def __init__(self, es, nc, name):
        self.h = es.enter_context(nc.semaphore(name))
        self.c = 0


class Ctx:
    def __init__(self, nc, cur):
        self.nc, self.cur = nc, cur

    def _fin(self, ins, inc):
        if inc is None:
            return
        sem, n = inc if isinstance(inc, tuple) else (inc, 1)
        if ins is not None:
            ins.then_inc(sem.h, n)
        sem.c += n

    def op(self, engine, fn, inc=None):
        ins = fn(getattr(self.nc, engine)) if self.cur == engine else None
        self._fin(ins, inc)

    def wait(self, engine, sem, val):
        if self.cur == engine and val > 0:
            getattr(self.nc, engine).wait_ge(sem.h, val)

    def drain(self, engine):
        if self.cur == engine:
            getattr(self.nc, engine).drain()

    def mm(self, out, lhsT, rhs, start, stop, inc=None):
        self.op('tensor', lambda e: e.matmul(out, lhsT, rhs, start=start,
                                             stop=stop), inc=inc)

    def dma(self, engine, out, in_, inc=None, transpose=False, accum=False):
        def f(e):
            if accum:
                return e.dma_start(out=out, in_=in_, accum_op=ALU.add)
            return e.dma_start(out=out, in_=in_, transpose=transpose)
        if inc is None:
            inc = self.dsem[engine]
        self.op(engine, f, inc=(inc, 16) if isinstance(inc, Sem) else inc)

    def barrier(self, sem):
        base = sem.c
        for e in ENGS:
            self.op(e, lambda eng: eng.nop(), inc=sem)
        for e in ENGS:
            self.wait(e, sem, base + len(ENGS))


class NS:
    pass


def build(debug_taps=()):
    nc = bass.Bass("TRN2", target_bir_lowering=False, debug=False,
                   detect_race_conditions=False)
    es = ExitStack()
    es.__enter__()
    B = NS()
    B.nc, B.es = nc, es

    din = lambda n, sh, dt_: nc.dram_tensor(n, sh, dt_,
                                            kind="ExternalInput").ap()
    dscr = lambda n, sh, dt_: nc.dram_tensor(n, sh, dt_, kind="Internal").ap()
    B.xq_d = din("xq", [R, D], F32)
    B.xk_d = din("xk", [T, D], F32)
    B.wqa_d = din("wqa", [D, CQ], F32)
    B.wqb_d = din("wqb", [CQ, H * DQK], F32)
    B.qnw_d = din("qnw", [128, 12], F32)
    B.wkva_d = din("wkva", [D, CK], F32)
    B.wkvb_d = din("wkvb", [CK, H * 256], F32)
    B.kvnw_d = din("kvnw", [128, 4], F32)
    B.wkr_d = din("wkr", [D, DR], F32)
    B.wo_d = din("wo", [D, D], F32)
    B.gw_d = din("gw", [DFF, D], I32)
    B.uw_d = din("uw", [DFF, D], I32)
    B.dw_d = din("dw", [D, DFF], I32)
    B.gsc_d = din("gsc", [128, 64], F32)
    B.usc_d = din("usc", [128, 64], F32)
    B.dsc_d = din("dsc", [128, 16], F32)
    B.ckt_d = din("ckt", [T, 32], F32)
    B.skt_d = din("skt", [T, 32], F32)
    B.cqr_d = din("cqr", [R, H * 32], F32)
    B.sqr_d = din("sqr", [R, H * 32], F32)
    B.mask_d = din("mask01", [128, KB], F32)
    B.cst_d = din("cst", [128, 16], F32)
    B.oneh_d = din("oneh", [128, 256], F32)
    B.out_d = nc.dram_tensor("out", [R, D], F32, kind="ExternalOutput").ap()

    B.ckvT_d = dscr("ckvT_scr", [128, 4 * T], BF16)
    B.x2_d = dscr("x2_scr", [128, NSUB * D], F32)
    B.yffn_d = dscr("yffn_scr", [128, NSUB * D], F32)
    B.otn0_d = dscr("otn0_scr", [128, HPH * R], BF16)

    B.taps = {}
    for name, shape, dt_ in debug_taps:
        B.taps[name] = nc.dram_tensor("tap_" + name, shape, dt_,
                                      kind="ExternalOutput").ap()

    arena_t = es.enter_context(nc.sbuf_tensor("arena", [128, AR], F32))
    B.bufF = lambda c0, w: arena_t[:, c0:c0 + w]
    B.bufB = lambda c0, w: arena_t[:, c0:c0 + (w + 1) // 2].bitcast(
        BF16)[:, 0:w]
    B.bufI = lambda c0, w: arena_t[:, c0:c0 + w].bitcast(I32)

    sbt = lambda n, sh, dt_: es.enter_context(
        nc.sbuf_tensor(n, sh, dt_))[:, :]
    B.cst = sbt("s_cst", [128, 16], F32)
    B.qnw = sbt("s_qnw", [128, 12], F32)
    B.kvnw = sbt("s_kvnw", [128, 4], F32)
    B.gsc = sbt("s_gsc", [128, 64], F32); B.gsc2 = sbt("gsc2", [128, 64], F32)
    B.usc = sbt("s_usc", [128, 64], F32); B.usc2 = sbt("usc2", [128, 64], F32)
    B.dsc = sbt("s_dsc", [128, 16], F32); B.dsc2 = sbt("dsc2", [128, 16], F32)
    B.mask01 = sbt("mask01s", [128, KB], BF16)
    B.ones_row = sbt("ones_row", [1, 128], F32)
    B.ones_bfc = sbt("ones_bfc", [128, 1], BF16)
    B.stats = sbt("stats", [128, 128], F32)
    B.statB = sbt("statB", [128, 128], F32)
    B.statC = sbt("statC", [128, 128], BF16)
    B.padA = sbt("padA", [128, 128], BF16)
    B.denP = sbt("denP", [128, 128], BF16)
    B.recD = sbt("recD", [128, 128], F32)
    B.tiny = sbt("tiny", [128, 32], F32)
    B.biasT = sbt("biasT", [128, 128], F32)
    B.dacc = sbt("dacc", [128, 144], F32)
    B.onehb = sbt("onehb", [128, 256], BF16)
    B.ropetmp = sbt("ropetmp", [128, 128], F32)
    B.maskA = sbt("maskA", [128, KB], F32)
    B.dbg = sbt("dbgbuf", [128, 1024], F32)
    B.den_scr = sbt("den_scrx", [128, 64], F32)
    B.cc = lambda j: B.cst[:, j:j + 1]

    B.S = {name: Sem(es, nc, name) for name in (
        "gl", "sl", "trA", "trB", "mmA", "vecA", "scaA", "ph")}
    B.psA = es.enter_context(nc.psum_tensor("psA", [128, 1024], F32))[:, :]
    B.psB = es.enter_context(nc.psum_tensor("psB", [128, 1024], F32))[:, :]
    B.psC = es.enter_context(nc.psum_tensor("psC", [128, 1024], F32))[:, :]
    B.psD = es.enter_context(nc.psum_tensor("psD", [128, 1024], F32))[:, :]
    return B


def tap(K, B, name, ap):
    if name in B.taps:
        ph, gl = B.S['ph'], B.S['gl']
        K.barrier(ph)
        K.dma(G, B.taps[name], ap, inc=gl)
        K.wait(G, gl, gl.c)
        K.barrier(ph)


def S_trB_warm(B):
    return B.S['trB']


def phase0(K, B):
    sl, vecA = B.S['sl'], B.S['vecA']
    cc = B.cc
    maskf = B.bufF(FR, KB)
    onehf = B.bufF(FR + 1024, 256)
    K.dma(SC, B.tiny[:, 0:16], B.cst_d, inc=S_trB_warm(B))
    K.wait(SC, B.S['sl'], B.S['sl'].c * 0 + 16)
    K.op(SC, lambda e: e.activation(B.tiny[0:1, 30:31], B.tiny[0:1, 0:1],
                                    AF.Sqrt), inc=B.S['scaA'])
    for dst, src in ((B.cst, B.cst_d), (B.qnw, B.qnw_d), (B.kvnw, B.kvnw_d),
                     (B.gsc, B.gsc_d), (B.usc, B.usc_d), (B.dsc, B.dsc_d),
                     (maskf, B.mask_d), (onehf, B.oneh_d)):
        K.dma(SY, dst, src, inc=sl)
    K.wait(VE, sl, sl.c)
    K.op(VE, lambda e: e.tensor_copy(B.onehb, onehf))
    for src, dst in ((B.gsc, B.gsc2), (B.usc, B.usc2), (B.dsc, B.dsc2)):
        K.op(VE, lambda e, src=src, dst=dst: e.tensor_scalar(
            dst, src, cc(C_M128), None, ALU.mult))
    K.op(VE, lambda e: e.tensor_copy(B.mask01, maskf))
    K.op(VE, lambda e: e.tensor_scalar(
        B.maskA, maskf, -1.0e30, 1.0e30, ALU.mult, ALU.add))
    K.op(VE, lambda e: e.memset(B.ones_row, 1.0))
    K.op(VE, lambda e: e.memset(B.ones_bfc, 1.0))
    for z in (B.stats, B.statB, B.statC, B.padA, B.denP, B.dacc):
        K.op(VE, lambda e, z=z: e.memset(z, 0.0))
    K.drain(VE)
    K.op(VE, lambda e: e.nop(), inc=vecA)


def phase1(K, B):
    S = B.S
    gl, sl, trA, trB = S['gl'], S['sl'], S['trA'], S['trB']
    mmA, vecA, scaA = S['mmA'], S['vecA'], S['scaA']
    cc, tiny, stats, statB = B.cc, B.tiny, B.stats, B.statB
    psA, psB, psC, psD = B.psA, B.psB, B.psC, B.psD

    wkva = B.bufB(FR + 0, 16 * CK)
    wkr = B.bufB(FR + 4096, 16 * DR)
    wkvb = B.bufB(FR + 4608, 4 * 4096)
    xkbf = B.bufB(FR + 12800, 2 * D)
    xkT = B.bufB(FR + 14848, 2 * D)
    ckvbf = B.bufB(FR + 16896, 2 * CK)
    ckvTs = B.bufB(FR + 17408, 4 * KB)
    krn = B.bufB(FR + 19456, 2 * 128)
    sqs = B.bufB(FR + 19584, CK)
    ckst = B.bufF(FR + 19840, 64)
    skst = B.bufF(FR + 19904, 64)
    krotT = B.bufB(A_KROT, T)
    stg = B.bufF(FR + 19968, 4096)

    cast_d = {-4: 0, -3: 0, -2: 0, -1: 0}
    for c in range(16):
        sA = stg[:, (c % 4) * 1024:(c % 4) * 1024 + 512]
        sB = stg[:, (c % 4) * 1024 + 512:(c % 4) * 1024 + 576]
        K.wait(SY, scaA, cast_d[c - 4])
        K.dma(SY, sA, B.wkva_d[c * 128:(c + 1) * 128, :])
        K.dma(SY, sB, B.wkr_d[c * 128:(c + 1) * 128, :], inc=sl)
        K.wait(SC, sl, sl.c)
        K.op(SC, lambda e, c=c, sA=sA: e.activation(
            wkva[:, c * CK:(c + 1) * CK], sA, AF.Copy))
        K.op(SC, lambda e, c=c, sB=sB: e.activation(
            wkr[:, c * DR:(c + 1) * DR], sB, AF.Copy), inc=scaA)
        cast_d[c] = scaA.c
    p1_w = scaA.c
    fold_d = {-1: 0}
    for c in range(4):
        if c == 0:
            K.wait(SY, scaA, p1_w)
        K.wait(SY, vecA, fold_d[c - 1])
        K.dma(SY, stg, B.wkvb_d[c * 128:(c + 1) * 128, :], inc=sl)
        K.wait(VE, sl, sl.c)
        K.op(VE, lambda e, c=c: e.tensor_scalar(
            wkvb[:, c * 4096:(c + 1) * 4096], stg,
            B.kvnw[:, c:c + 1], None, ALU.mult), inc=vecA)
        fold_d[c] = vecA.c
    K.op(VE, lambda e: e.memset(krn, 0.0), inc=vecA)
    p1_fold = vecA.c

    tr_d, ld_d, ck_d, mm_d, xc_d = {}, {}, {}, {}, {-2: 0, -1: 0}
    st_d = {-4: 0, -3: 0, -2: 0, -1: 0}
    ev_d = {-4: 0, -3: 0, -2: 0, -1: 0}
    ctr_d = {-2: 0, -1: 0}
    sp_d = {-1: 0}
    for t in range(64):
        b, tin, pp = t // 8, t % 8, t % 2
        lq, lsem = (G, gl) if pp == 0 else (SY, sl)
        if t < 2:
            K.wait(lq, vecA, p1_fold)
        K.wait(lq, scaA, xc_d[t - 2])
        K.dma(lq, stg[:, pp * 2048:(pp + 1) * 2048],
              B.xk_d[t * 128:(t + 1) * 128, :], inc=lsem)
        ld_d[t] = (lsem, lsem.c)
        K.wait(SY, vecA, ev_d[t - 2])
        K.dma(SY, ckst[:, pp * 32:(pp + 1) * 32],
              B.ckt_d[t * 128:(t + 1) * 128, :])
        K.dma(SY, skst[:, pp * 32:(pp + 1) * 32],
              B.skt_d[t * 128:(t + 1) * 128, :], inc=sl)
        ck_d[t] = sl.c
        K.wait(SC, ld_d[t][0], ld_d[t][1])
        if t >= 2:
            K.wait(SC, trA, tr_d[t - 2])
        K.op(SC, lambda e, pp=pp: e.activation(
            xkbf[:, pp * D:(pp + 1) * D], stg[:, pp * 2048:(pp + 1) * 2048],
            AF.Copy), inc=scaA)
        xc_d[t] = scaA.c
        if t >= 2:
            K.wait(SC, mmA, mm_d[t - 2])
        K.dma(SC, wtr(xkT, pp * D, 16, 128, 128),
              xkbf[:, pp * D:(pp + 1) * D], inc=trA, transpose=True)
        tr_d[t] = trA.c
        ps = (psA, psB, psC, psD)[t % 4]
        K.wait(TE, trA, tr_d[t])
        if t == 0:
            K.wait(TE, scaA, p1_w)
            K.wait(TE, vecA, p1_fold)
        K.wait(TE, scaA, st_d[t - 4])
        K.wait(TE, vecA, ev_d[t - 4])
        for c in range(16):
            lhs = xkT[:, pp * D + c * 128: pp * D + (c + 1) * 128]
            K.mm(ps[:, 0:CK], lhs, wkva[:, c * CK:(c + 1) * CK],
                 start=(c == 0), stop=(c == 15))
        for c in range(16):
            lhs = xkT[:, pp * D + c * 128: pp * D + (c + 1) * 128]
            K.mm(ps[:, 512:576], lhs, wkr[:, c * DR:(c + 1) * DR],
                 start=(c == 0), stop=(c == 15),
                 inc=mmA if c == 15 else None)
        mm_d[t] = mmA.c
        K.wait(SC, mmA, mm_d[t])
        K.op(SC, lambda e, ps=ps, tin=tin: e.activation(
            sqs, ps[:, 0:CK], AF.Square, accum_out=stats[:, tin:tin + 1]))
        K.op(SC, lambda e, ps=ps, t=t: e.activation(
            sqs[:, 0:DR], ps[:, 512:576], AF.Square,
            accum_out=statB[:, t:t + 1]), inc=scaA)
        st_d[t] = scaA.c
        K.wait(VE, scaA, st_d[t])
        K.wait(VE, sl, ck_d[t])
        K.op(VE, lambda e, tin=tin: e.tensor_scalar(
            tiny[:, 0:1], stats[:, tin:tin + 1], cc(C_I512), cc(C_EPSR),
            ALU.mult, ALU.add), inc=vecA)
        K.wait(SC, vecA, vecA.c)
        K.op(SC, lambda e: e.activation(tiny[:, 1:2], tiny[:, 0:1], AF.Sqrt),
             inc=scaA)
        K.wait(VE, scaA, scaA.c)
        K.op(VE, lambda e: e.reciprocal(tiny[:, 2:3], tiny[:, 1:2]))
        K.drain(VE)
        if t >= 2:
            K.wait(VE, trB, ctr_d[t - 2])
        K.op(VE, lambda e, ps=ps, pp=pp: e.tensor_scalar(
            ckvbf[:, pp * CK:(pp + 1) * CK], ps[:, 0:CK],
            tiny[:, 2:3], None, ALU.mult))
        x1, x2_ = ps[:, 512:544], ps[:, 544:576]
        ct = ckst[:, pp * 32:(pp + 1) * 32]
        st_ = skst[:, pp * 32:(pp + 1) * 32]
        kn = krn[:, pp * 128: pp * 128 + 64]
        K.op(VE, lambda e, x1=x1, ct=ct: e.tensor_mul(stats[:, 64:96], x1, ct))
        K.op(VE, lambda e, x2_=x2_, st_=st_: e.tensor_mul(
            stats[:, 96:128], x2_, st_))
        K.op(VE, lambda e, x2_=x2_, ct=ct: e.tensor_mul(
            B.statB[:, 64:96], x2_, ct))
        K.op(VE, lambda e, x1=x1, st_=st_: e.tensor_mul(
            B.statB[:, 96:128], x1, st_))
        K.drain(VE)
        K.op(VE, lambda e, kn=kn: e.tensor_sub(
            kn[:, 0:32], stats[:, 64:96], stats[:, 96:128]))
        K.op(VE, lambda e, kn=kn: e.tensor_add(
            kn[:, 32:64], B.statB[:, 64:96], B.statB[:, 96:128]))
        K.drain(VE)
        K.op(VE, lambda e, pp=pp: e.tensor_copy(
            krn[:, pp * 128 + 64: pp * 128 + 128],
            krn[:, pp * 128: pp * 128 + 64]), inc=vecA)
        ev_d[t] = vecA.c
        K.wait(SC, vecA, ev_d[t])
        if tin == 0:
            K.wait(SC, gl, sp_d[b - 1])
        K.dma(SC, wtr(ckvTs, tin * 128, 4, KB, 128),
              ckvbf[:, pp * CK:(pp + 1) * CK], transpose=True)
        K.dma(SC, krotT[:, t * 128:(t + 1) * 128],
              krn[:, pp * 128:(pp + 1) * 128], inc=trB, transpose=True)
        ctr_d[t] = trB.c
        if tin == 7:
            K.wait(G, trB, ctr_d[t])
            for c in range(4):
                K.dma(G, B.ckvT_d[:, c * T + b * KB: c * T + (b + 1) * KB],
                      ckvTs[:, c * KB:(c + 1) * KB],
                      inc=gl if c == 3 else None)
            sp_d[b] = gl.c
    K.wait(VE, scaA, scaA.c)
    K.op(VE, lambda e: e.reduce_max(tiny[:, 3:4], statB[:, 0:64], AX.X))
    K.drain(VE)
    K.op(VE, lambda e: e.tensor_copy(B.statC[:, 0:1], tiny[:, 3:4]),
         inc=vecA)
    K.wait(SC, vecA, vecA.c)
    K.dma(SC, B.padA, B.statC, inc=trB, transpose=True)
    K.wait(VE, trB, trB.c)
    K.op(VE, lambda e: e.reduce_max(tiny[:, 4:5], B.padA[:, 0:128],
                                    AX.X))
    K.op(VE, lambda e: e.tensor_copy(B.denP, B.statC), inc=vecA)
    K.wait(SC, vecA, vecA.c)
    K.op(SC, lambda e: e.activation(tiny[0:1, 5:6], tiny[0:1, 4:5], AF.Sqrt),
         inc=scaA)
    K.wait(TE, scaA, scaA.c)
    K.wait(TE, vecA, ev_d[63])
    K.mm(psC[:, 0:1], B.ones_row[0:1, 0:128], tiny[0:1, 5:6],
         start=True, stop=True, inc=mmA)
    K.wait(VE, mmA, mmA.c)
    K.op(VE, lambda e: e.tensor_copy(tiny[:, 5:6], psC[:, 0:1]), inc=vecA)
    mk_done = vecA.c
    g_d = {-2: mk_done, -1: mk_done}
    for h in range(H):
        ps = psC if h % 2 == 0 else psD
        K.wait(TE, vecA, g_d[h - 2])
        for c in range(4):
            w = wkvb[:, c * 4096 + h * 256: c * 4096 + h * 256 + 128]
            K.mm(ps[:, 0:128], w, w, start=(c == 0), stop=(c == 3),
                 inc=mmA if c == 3 else None)
        K.wait(VE, mmA, mmA.c)
        K.op(VE, lambda e, h=h, ps=ps: e.reduce_sum(
            statB[:, h:h + 1], ps[:, 0:128], AX.X,
            apply_absolute_value=True), inc=vecA)
        g_d[h] = vecA.c
    K.drain(VE)
    K.op(VE, lambda e: e.tensor_copy(B.statC[:, 0:16], statB[:, 0:16]),
         inc=vecA)
    K.wait(SC, vecA, vecA.c)
    K.dma(SC, B.padA, B.statC, inc=trB, transpose=True)
    K.wait(VE, trB, trB.c)
    K.op(VE, lambda e: e.reduce_max(tiny[:, 7:8], B.padA[:, 0:128], AX.X),
         inc=vecA)
    K.wait(SC, vecA, vecA.c)
    K.op(SC, lambda e: e.activation(tiny[:, 8:9], tiny[:, 7:8], AF.Sqrt),
         inc=scaA)
    K.wait(VE, scaA, scaA.c)
    K.drain(VE)
    K.op(VE, lambda e: e.tensor_scalar(tiny[:, 9:10], tiny[:, 5:6],
                                       cc(C_SQK), None, ALU.mult))
    K.drain(VE)
    K.op(VE, lambda e: e.scalar_tensor_tensor(
        tiny[:, 10:11], tiny[:, 8:9], cc(C_B512), tiny[:, 9:10],
        ALU.mult, ALU.add))
    K.drain(VE)
    K.op(VE, lambda e: e.nop(), inc=vecA)


def phase2(K, B):
    S = B.S
    gl, sl, trA, trB = S['gl'], S['sl'], S['trA'], S['trB']
    mmA, vecA, scaA = S['mmA'], S['vecA'], S['scaA']
    cc, tiny, stats, statB = B.cc, B.tiny, B.stats, B.statB
    psA, psB, psC, psD = B.psA, B.psB, B.psC, B.psD

    qaT = B.bufB(FR + 0, 12 * R)
    wqa = B.bufB(FR + 6144, 16 * CQ)
    wqb = B.bufB(FR + 6144, 12 * 1536)
    xqbf = B.bufB(FR + 18432, 2 * D)
    xqT = B.bufB(FR + 20480, 2 * D)
    qabf = B.bufB(FR + 22528, CQ)
    qtr_buf = B.bufB(FR + 23296, 1024)
    qps = B.bufB(FR + 23296, 1024)
    cqs = B.bufF(FR + 18432, 256)
    sqt = B.bufF(FR + 18688, 256)
    rpak = B.bufB(FR + 24160, 4 * 128 + 64)
    st0 = B.bufF(16384, 2048)
    st1 = B.bufF(18432, 2048)
    qfT = B.bufB(A_QFT, 24 * R)
    B.qfT = qfT

    wc_d = {-2: 0, -1: 0}
    for c in range(16):
        sA = (st0 if c % 2 == 0 else st1)[:, 0:1536]
        lq, lsem = (G, gl) if c % 2 == 0 else (SY, sl)
        K.wait(lq, scaA, wc_d[c - 2])
        K.dma(lq, sA, B.wqa_d[c * 128:(c + 1) * 128, :], inc=lsem)
        K.wait(SC, lsem, lsem.c)
        K.op(SC, lambda e, c=c, sA=sA: e.activation(
            wqa[:, c * CQ:(c + 1) * CQ], sA, AF.Copy), inc=scaA)
        wc_d[c] = scaA.c
    p2_w = scaA.c
    xtr_d, xl_d, qa_mm = {}, {}, {}
    qa_ev = {-2: 0, -1: 0}
    qa_vv = {}
    qatr = {}
    xc_d = {-2: wc_d[15], -1: wc_d[15]}
    for s in range(NSUB):
        pp = s % 2
        st_ = st0 if pp == 0 else st1
        K.wait(SY, scaA, xc_d[s - 2])
        K.dma(SY, st_, B.xq_d[s * 128:(s + 1) * 128, :], inc=sl)
        xl_d[s] = sl.c
        K.wait(SC, sl, xl_d[s])
        if s >= 2:
            K.wait(SC, trA, xtr_d[s - 2])
        K.op(SC, lambda e, pp=pp, st_=st_: e.activation(
            xqbf[:, pp * D:(pp + 1) * D], st_, AF.Copy), inc=scaA)
        xc_d[s] = scaA.c
        if s >= 2:
            K.wait(SC, mmA, qa_mm[s - 2])
        K.dma(SC, wtr(xqT, pp * D, 16, 128, 128),
              xqbf[:, pp * D:(pp + 1) * D], inc=trA, transpose=True)
        xtr_d[s] = trA.c
        p0 = psA if pp == 0 else psC
        p1_ = psB if pp == 0 else psD
        K.wait(TE, trA, xtr_d[s])
        if s == 0:
            K.wait(TE, gl, p2_w)
        K.wait(TE, scaA, qa_ev[s - 2])
        for n in range(3):
            dst = p0[:, n * 512:(n + 1) * 512] if n < 2 else p1_[:, 0:512]
            for c in range(16):
                K.mm(dst, xqT[:, pp * D + c * 128: pp * D + (c + 1) * 128],
                     wqa[:, c * CQ + n * 512: c * CQ + (n + 1) * 512],
                     start=(c == 0), stop=(c == 15),
                     inc=mmA if (c == 15 and n == 2) else None)
        qa_mm[s] = mmA.c
        tb = 12 + (s % 2) * 8
        K.wait(SC, mmA, qa_mm[s])
        K.wait(SC, vecA, qa_vv.get(s - 2, 0))
        K.op(SC, lambda e, p0=p0, tb=tb: e.activation(
            qtr_buf[:, 0:KB], p0[:, 0:KB], AF.Square,
            accum_out=tiny[:, tb:tb + 1]))
        K.op(SC, lambda e, p1_=p1_, tb=tb: e.activation(
            qtr_buf[:, 0:512], p1_[:, 0:512], AF.Square,
            accum_out=tiny[:, tb + 1:tb + 2]), inc=scaA)
        K.wait(VE, scaA, scaA.c)
        K.op(VE, lambda e, tb=tb: e.tensor_add(
            tiny[:, tb + 2:tb + 3], tiny[:, tb:tb + 1],
            tiny[:, tb + 1:tb + 2]))
        K.drain(VE)
        K.op(VE, lambda e, tb=tb: e.tensor_scalar(
            tiny[:, tb + 2:tb + 3], tiny[:, tb + 2:tb + 3], cc(C_I1536),
            cc(C_EPSR), ALU.mult, ALU.add), inc=vecA)
        K.wait(SC, vecA, vecA.c)
        K.op(SC, lambda e, tb=tb: e.activation(
            tiny[:, tb + 3:tb + 4], tiny[:, tb + 2:tb + 3], AF.Sqrt),
            inc=scaA)
        K.wait(VE, scaA, scaA.c)
        K.op(VE, lambda e, tb=tb: e.reciprocal(tiny[:, tb + 4:tb + 5],
                                               tiny[:, tb + 3:tb + 4]))
        K.drain(VE)
        K.wait(VE, trB, qatr.get(s - 1, 0))
        K.op(VE, lambda e, p0=p0, tb=tb: e.tensor_scalar(
            qabf[:, 0:KB], p0[:, 0:KB], tiny[:, tb + 4:tb + 5], None,
            ALU.mult))
        K.op(VE, lambda e, p1_=p1_, tb=tb: e.tensor_scalar(
            qabf[:, KB:CQ], p1_[:, 0:512], tiny[:, tb + 4:tb + 5], None,
            ALU.mult), inc=vecA)
        qa_evv = vecA.c
        qa_vv[s] = vecA.c
        K.wait(SC, vecA, qa_evv)
        K.op(SC, lambda e: e.nop(), inc=scaA)
        qa_ev[s] = scaA.c
        K.wait(SC, vecA, qa_evv)
        K.dma(SC, wtr(qaT, s * 128, 12, R, 128), qabf[:, 0:CQ],
              inc=trB, transpose=True)
        qatr[s] = trB.c
    p2_qaT = trB.c

    q_ev = {}
    qtr_prev = 0
    qf_prev = [0]
    wq_d = {}
    for hf in range(2):
        mm_snap = mmA.c
        for c in range(12):
            st_ = (st0 if c % 2 == 0 else st1)[:, 0:1536]
            lq, lsem = (G, gl) if c % 2 == 0 else (SY, sl)
            K.wait(lq, scaA, xc_d[NSUB - 1] if (hf == 0 and c < 2) else 0)
            K.wait(lq, vecA, wq_d.get((hf, c - 2),
                                      wq_d.get((hf - 1, c + 10), 0)))
            K.dma(lq, st_, B.wqb_d[c * 128:(c + 1) * 128,
                                   hf * 1536:(hf + 1) * 1536], inc=lsem)
            K.wait(VE, lsem, lsem.c)
            if c == 0:
                K.wait(VE, mmA, mm_snap)
            K.op(VE, lambda e, c=c, st_=st_: e.tensor_scalar(
                wqb[:, c * 1536:(c + 1) * 1536], st_,
                B.qnw[:, c:c + 1], None, ALU.mult), inc=vecA)
            wq_d[(hf, c)] = vecA.c
        p2_fold = vecA.c
        for s in range(NSUB):
            pp = s % 2
            p0 = psA if pp == 0 else psC
            p1_ = psB if pp == 0 else psD
            if hf == 0 and s == 0:
                K.wait(TE, trB, p2_qaT)
            if s == 0:
                K.wait(TE, vecA, p2_fold)
            K.wait(TE, scaA, q_ev.get((hf, s - 2),
                                      q_ev.get((hf - 1, s + 6), 0)))
            for n in range(3):
                dst = p0[:, n * 512:(n + 1) * 512] if n < 2 else p1_[:, 0:512]
                for c in range(12):
                    K.mm(dst, qaT[:, c * R + s * 128: c * R + (s + 1) * 128],
                         wqb[:, c * 1536 + n * 512: c * 1536 + (n + 1) * 512],
                         start=(c == 0), stop=(c == 11),
                         inc=mmA if (c == 11 and n == 2) else None)
            q_mm = mmA.c
            K.wait(SY, vecA, qf_prev[0])
            K.dma(SY, cqs, B.cqr_d[s * 128:(s + 1) * 128,
                                   hf * 256:(hf + 1) * 256])
            K.dma(SY, sqt, B.sqr_d[s * 128:(s + 1) * 128,
                                   hf * 256:(hf + 1) * 256], inc=sl)
            rp_ld = sl.c
            K.wait(VE, mmA, q_mm)
            K.wait(VE, sl, rp_ld)
            K.wait(VE, trB, qtr_prev)
            K.op(VE, lambda e, p0=p0: e.tensor_copy(
                win(qps, 0, 5, 128, 128), win(p0, 0, 5, 192, 128)))
            K.op(VE, lambda e, p0=p0: e.tensor_copy(
                qps[:, 640:704], p0[:, 960:1024]))
            K.op(VE, lambda e, p1_=p1_: e.tensor_copy(
                qps[:, 704:768], p1_[:, 0:64]))
            K.op(VE, lambda e, p1_=p1_: e.tensor_copy(
                win(qps, 768, 2, 128, 128), win(p1_, 128, 2, 192, 128)))

            def rope(pt, rel0, l0, nl):
                c3 = cqs.rearrange("p (l f) -> p l f", f=32)[:, l0:l0 + nl]
                s3 = sqt.rearrange("p (l f) -> p l f", f=32)[:, l0:l0 + nl]
                x1v = win(pt, rel0, nl, 192, 32)
                x2v = win(pt, rel0 + 32, nl, 192, 32)
                o1 = win(rpak, l0 * 64, nl, 64, 32)
                o2 = win(rpak, l0 * 64 + 32, nl, 64, 32)
                t1 = win(stats, 0, nl, 32, 32)
                t2 = win(statB, 0, nl, 32, 32)
                t3 = win(B.recD, 0, nl, 32, 32)
                t4 = win(B.ropetmp, 0, nl, 32, 32)
                K.op(VE, lambda e: e.tensor_mul(t1, x1v, c3))
                K.op(VE, lambda e: e.tensor_mul(t2, x2v, s3))
                K.op(VE, lambda e: e.tensor_mul(t3, x2v, c3))
                K.op(VE, lambda e: e.tensor_mul(t4, x1v, s3))
                K.drain(VE)
                K.op(VE, lambda e: e.tensor_sub(o1, t1, t2))
                K.op(VE, lambda e: e.tensor_add(o2, t3, t4))

            rope(p0, 128, 0, 4)
            rope(p0, 4 * 192 + 128, 4, 1)
            rope(p1_, 64, 5, 3)
            K.drain(VE)
            K.op(VE, lambda e: e.nop(), inc=vecA)
            qf_ev = vecA.c
            qf_prev[0] = vecA.c
            K.wait(SC, vecA, qf_ev)
            K.op(SC, lambda e: e.nop(), inc=scaA)
            q_ev[(hf, s)] = scaA.c
            K.wait(SC, vecA, qf_ev)
            K.dma(SC, wtr(qfT, (hf * 8) * R + s * 128, 8, R, 128),
                  qps, transpose=True)
            K.dma(SC, wtr(qfT, (16 + hf * 4) * R + s * 128, 4, R, 128),
                  rpak[:, 0:512], inc=trB, transpose=True)
            qtr_prev = trB.c
    B.p2_qfT_done = trB.c


def phase25(K, B):
    S = B.S
    sl, trB = S['sl'], S['trB']
    mmA, vecA, scaA = S['mmA'], S['vecA'], S['scaA']
    cc, tiny = B.cc, B.tiny
    psA, psB = B.psA, B.psB
    qfT = B.qfT
    trash = B.bufB(FR + 18432, KB)
    trash2 = B.bufB(FR + 19456, KB)
    sumq = B.bufF(FR + 0, KB)
    mbf = B.bufB(FR + 2048, KB)

    mm_h = {-1: 0}
    for h in range(H):
        bb = (h % 2) * 64
        rc = 16 + h // 2
        K.wait(SC, trB, B.p2_qfT_done)
        K.wait(SC, mmA, mm_h[h - 1])
        K.op(SC, lambda e, h=h: e.activation(
            trash, qfT[:, h * R:(h + 1) * R], AF.Square))
        K.op(SC, lambda e, rc=rc, bb=bb: e.activation(
            trash2[bb:bb + 64, :],
            qfT[bb:bb + 64, rc * R:(rc + 1) * R], AF.Square), inc=scaA)
        K.wait(TE, scaA, scaA.c)
        for nh in range(2):
            K.mm(psA[0:16, nh * 512:(nh + 1) * 512],
                 B.onehb[0:128, h * 16:(h + 1) * 16],
                 trash[:, nh * 512:(nh + 1) * 512],
                 start=(h == 0), stop=False)
            K.mm(psA[0:16, nh * 512:(nh + 1) * 512],
                 B.onehb[bb:bb + 64, h * 16:(h + 1) * 16],
                 trash2[bb:bb + 64, nh * 512:(nh + 1) * 512],
                 start=False, stop=(h == 15),
                 inc=mmA if nh == 1 else None)
        mm_h[h] = mmA.c
    K.wait(VE, mmA, mm_h[H - 1])
    K.op(VE, lambda e: e.tensor_copy(sumq[0:16, :], psA[0:16, 0:R]),
         inc=vecA)
    K.wait(SC, vecA, vecA.c)
    K.op(SC, lambda e: e.activation(sumq[0:16, :], sumq[0:16, :], AF.Sqrt),
         inc=scaA)
    K.wait(VE, scaA, scaA.c)
    K.op(VE, lambda e: e.tensor_scalar(
        sumq[0:16, :], sumq[0:16, :], tiny[0:16, 10:11], None, ALU.mult))
    K.drain(VE)
    K.op(VE, lambda e: e.tensor_scalar(
        sumq[0:16, :], sumq[0:16, :], cc(C_NEG1)[0:16, :],
        cc(C_SHIFT)[0:16, :], ALU.mult, ALU.add))
    K.drain(VE)
    K.op(VE, lambda e: e.tensor_copy(mbf[0:16, :], sumq[0:16, :]), inc=vecA)
    K.wait(SC, vecA, vecA.c)
    for j in range(8):
        K.dma(SC, B.denP[:, j * 16:(j + 1) * 16],
              mbf[0:16, j * 128:(j + 1) * 128],
              inc=trB if j == 7 else None, transpose=True)
    K.wait(VE, trB, trB.c)
    K.op(VE, lambda e: e.tensor_copy(B.biasT, B.denP))
    K.drain(VE)
    K.op(VE, lambda e: e.memset(B.denP, 0.0), inc=vecA)


def phase3(K, B):
    S = B.S
    gl, sl, trA, trB = S['gl'], S['sl'], S['trA'], S['trB']
    mmA, vecA, scaA = S['mmA'], S['vecA'], S['scaA']
    cc = B.cc
    psA, psB, psC, psD = B.psA, B.psB, B.psC, B.psD
    qfT = B.qfT
    krotT = B.bufB(A_KROT, T)
    kpT = B.bufB(FR + 8192, HPH * KB)
    vbf = B.bufB(FR + 12288, NSUB * KB)
    pT = B.bufB(FR + 16384, NSUB * KB)
    pscr = B.bufB(FR + 20480, 4 * KB)
    ckv = B.bufB(FR + 22528, 4 * KB)
    wkvbh = B.bufB(A_OTN, 4 * 2048)
    wkvbx = B.bufB(A_OTN, 4 * 2048 + 128)
    oTn = B.bufB(A_OTN, HPH * R)
    oTf = lambda hl, q0, w: B.bufF(FR + hl * 1024 + q0, w)

    K.op(SC, lambda e: e.activation(B.tiny[0:1, 30:31], B.cst[0:1, 8:9],
                                    AF.Exp), inc=B.S['scaA'])
    slot_exp = [0, 0]
    slot_tr = [0, 0, 0, 0]
    fold_prev = 0
    otn_half_done = 0
    spill_gl = 0
    ckvstg = B.bufF(FR + 22528, 2048)
    for hf in range(2):
        wf_d = {-1: 0}
        for c in range(4):
            K.wait(SY, vecA, max(wf_d[c - 1], otn_half_done))
            K.wait(SY, gl, spill_gl)
            K.dma(SY, ckvstg, B.wkvb_d[c * 128:(c + 1) * 128,
                                       hf * 2048:(hf + 1) * 2048], inc=sl)
            K.wait(VE, sl, sl.c)
            if c == 0:
                K.wait(VE, vecA, otn_half_done)
            K.op(VE, lambda e, c=c: e.tensor_scalar(
                wkvbh[:, c * 2048:(c + 1) * 2048], ckvstg,
                B.kvnw[:, c:c + 1], None, ALU.mult), inc=vecA)
            wf_d[c] = vecA.c
        wkvb_ok = vecA.c
        for s8 in range(8):
            K.op(VE, lambda e, s8=s8: e.memset(oTf(s8, 0, 1024), 0.0))
        K.op(VE, lambda e: e.nop(), inc=vecA)
        pv_prev = vecA.c

        kvmm_prev = 0
        pvmm_prev = 0
        for b in range(DBG_NB or NB):
            K.wait(G, mmA, kvmm_prev)
            if b == 0:
                K.wait(G, vecA, wkvb_ok)
            for c in range(4):
                K.dma(G, ckv[:, c * KB:(c + 1) * KB],
                      B.ckvT_d[:, c * T + b * KB: c * T + (b + 1) * KB],
                      inc=gl if c == 3 else None)
            ckv_ld = gl.c
            ev0 = {-2: 0, -1: 0}
            for hl in range(HPH):
                ps = psC if hl % 2 == 0 else psD
                K.wait(TE, gl, ckv_ld)
                if hf == 0 and b == 0 and hl == 0:
                    K.wait(TE, vecA, wkvb_ok)
                if hl == 0:
                    K.wait(TE, vecA, pv_prev)
                K.wait(TE, scaA, ev0[hl - 2])
                for c in range(4):
                    w = wkvbh[:, c * 2048 + hl * 256:
                              c * 2048 + hl * 256 + 128]
                    for nh in range(2):
                        K.mm(ps[:, nh * 512:(nh + 1) * 512], w,
                             ckv[:, c * KB + nh * 512: c * KB + (nh + 1) * 512],
                             start=(c == 0), stop=(c == 3),
                             inc=mmA if (c == 3 and nh == 1) else None)
                K.wait(SC, mmA, mmA.c)
                K.op(SC, lambda e, hl=hl, ps=ps: e.activation(
                    kpT[:, hl * KB:(hl + 1) * KB], ps, AF.Copy), inc=scaA)
                ev0[hl] = scaA.c
            vev = {-2: ev0[HPH - 2], -1: ev0[HPH - 1]}
            for s8 in range(NSUB):
                ps = psC if s8 % 2 == 0 else psD
                K.wait(TE, scaA, vev[s8 - 2])
                for c in range(4):
                    lhs = ckv[:, c * KB + s8 * 128: c * KB + (s8 + 1) * 128]
                    rhs = win(wkvbx, c * 2048 + 128, HPH, 256, 128)
                    for nh in range(2):
                        K.mm(ps[:, nh * 512:(nh + 1) * 512], lhs,
                             rhs[:, nh * 4:(nh + 1) * 4, :],
                             start=(c == 0), stop=(c == 3),
                             inc=mmA if (c == 3 and nh == 1) else None)
                K.wait(SC, mmA, mmA.c)
                K.op(SC, lambda e, s8=s8, ps=ps: e.activation(
                    vbf[:, s8 * KB:(s8 + 1) * KB], ps, AF.Copy), inc=scaA)
                vev[s8] = scaA.c
            kv_ev = scaA.c
            kvmm_prev = mmA.c
            for hl in range(HPH):
                hg = hf * HPH + hl
                rc = 16 + hg // 2
                bb = (hg % 2) * 64
                sq = (NSUB - b) * 128
                inst_tr = {SC: 0, SY: 0}
                K.wait(SC, mmA, pvmm_prev)
                for mi in range(b, NSUB):
                    inst = hl * (NSUB - b) + (mi - b)
                    ps = psA if inst % 2 == 0 else psB
                    K.wait(TE, scaA, kv_ev if inst == 0 else 0)
                    K.wait(TE, scaA, slot_exp[inst % 2])
                    for nh in range(2):
                        K.mm(ps[:, nh * 512:(nh + 1) * 512],
                             qfT[:, hg * R + mi * 128:
                                 hg * R + (mi + 1) * 128],
                             kpT[:, hl * KB + nh * 512:
                                 hl * KB + (nh + 1) * 512],
                             start=True, stop=False)
                        K.mm(ps[:, nh * 512:(nh + 1) * 512],
                             qfT[bb:bb + 64,
                                 rc * R + mi * 128: rc * R + (mi + 1) * 128],
                             krotT[bb:bb + 64,
                                   b * KB + nh * 512: b * KB + (nh + 1) * 512],
                             start=False, stop=True,
                             inc=mmA if nh == 1 else None)
                    sc_mm = mmA.c
                    slot = (hl * 8 + (mi - b)) % 4
                    qn, sm = SC, trB
                    if mi == b:
                        K.wait(VE, mmA, sc_mm)
                        K.op(VE, lambda e, ps=ps: e.tensor_sub(
                            ps, ps, B.maskA), inc=vecA)
                        K.wait(SC, vecA, vecA.c)
                    K.wait(SC, mmA, sc_mm)
                    K.wait(SC, sm, slot_tr[slot])
                    if inst == 0:
                        K.wait(SC, vecA, fold_prev)
                    K.op(SC, lambda e, slot=slot, ps=ps, mi=mi, hg=hg, hl=hl:
                         e.activation(
                             pscr[:, slot * KB:(slot + 1) * KB], ps, AF.Exp,
                             bias=B.biasT[:, mi * 16 + hg: mi * 16 + hg + 1],
                             scale=cc(C_SQK),
                             accum_out=B.den_scr[:, mi * 8 + hl:
                                                 mi * 8 + hl + 1]),
                         inc=scaA)
                    slot_exp[inst % 2] = scaA.c
                    K.wait(qn, scaA, scaA.c)
                    K.dma(qn, wtr(pT, (mi - b) * 128, 8, KB, 128),
                          pscr[:, slot * KB:(slot + 1) * KB],
                          inc=sm, transpose=True)
                    slot_tr[slot] = sm.c
                    inst_tr[qn] = sm.c
                K.wait(TE, trB, inst_tr[SC])
                K.wait(TE, vecA, pv_prev)
                for nq in range((sq + 511) // 512):
                    w_ = min(512, sq - nq * 512)
                    for ks in range(8):
                        K.mm(psC[:, nq * 512: nq * 512 + w_],
                             vbf[:, ks * KB + hl * 128:
                                 ks * KB + (hl + 1) * 128],
                             pT[:, ks * KB + nq * 512:
                                ks * KB + nq * 512 + w_],
                             start=(ks == 0), stop=(ks == 7),
                             inc=mmA if ks == 7 else None)
                pvmm_prev = mmA.c
                K.wait(VE, mmA, mmA.c)
                K.drain(VE)
                K.op(VE, lambda e, hl=hl, b=b, sq=sq: e.tensor_add(
                    oTf(hl, b * 128, sq), oTf(hl, b * 128, sq),
                    psC[:, 0:sq]))
                K.drain(VE)
                K.op(VE, lambda e: e.nop(), inc=vecA)
                pv_prev = vecA.c
            K.wait(VE, scaA, scaA.c)
            K.drain(VE)
            src = win(B.den_scr, b * 8, NSUB - b, 8, 8)
            dst = win(B.dacc, b * 16 + hf * 8, NSUB - b, 16, 8)
            K.op(VE, lambda e, src=src, dst=dst: e.tensor_add(dst, dst, src))
            K.drain(VE)
            K.op(VE, lambda e: e.nop(), inc=vecA)
            fold_prev = vecA.c
        obf = B.bufB(FR + 20480, KB)
        onat = B.bufB(FR + 20992, KB)
        onat2 = B.bufB(FR + 21504, KB)
        K.op(VE, lambda e, hf=hf: e.reciprocal(
            win(B.recD, 0, 8, 8, 8), win(B.dacc, hf * 8, 8, 16, 8)))
        K.drain(VE)
        K.op(VE, lambda e: e.nop(), inc=vecA)
        rec_ok = vecA.c
        tr1 = {-1: 0}
        ts_d = {-1: 0}
        tr2 = {-1: 0}
        for hl in range(HPH):
            K.wait(SC, sl, tr1[hl - 1])
            if hl == 0:
                K.wait(SC, vecA, fold_prev)
            K.op(SC, lambda e, hl=hl: e.activation(
                obf, oTf(hl, 0, KB), AF.Copy), inc=scaA)
            K.wait(SY, scaA, scaA.c)
            K.wait(SY, vecA, ts_d[hl - 1])
            K.dma(SY, wtr(onat, 0, 8, 128, 128), obf, inc=sl,
                  transpose=True)
            tr1[hl] = sl.c
            K.wait(VE, sl, tr1[hl])
            if hl == 0:
                K.wait(VE, vecA, rec_ok)
            K.wait(VE, sl, tr2[hl - 1])
            for mi in range(NSUB):
                K.op(VE, lambda e, hl=hl, mi=mi: e.tensor_scalar(
                    onat2[:, mi * 128:(mi + 1) * 128],
                    onat[:, mi * 128:(mi + 1) * 128],
                    B.recD[:, mi * 8 + hl: mi * 8 + hl + 1], None,
                    ALU.mult))
            K.op(VE, lambda e: e.nop(), inc=vecA)
            ts_d[hl] = vecA.c
            K.wait(SY, vecA, ts_d[hl])
            K.dma(SY, wtr(oTn, hl * R, 8, 128, 128), onat2, inc=sl,
                  transpose=True)
            tr2[hl] = sl.c
        K.wait(VE, sl, tr2[HPH - 1])
        K.op(VE, lambda e: e.nop(), inc=vecA)
        otn_half_done = vecA.c
        if hf == 0:
            K.wait(G, vecA, otn_half_done)
            K.dma(G, B.otn0_d, oTn, inc=gl)
            spill_gl = gl.c
            tap(K, B, 'oTn0s', oTn)
            tap(K, B, 'recD0', B.recD)
            tap(K, B, 'dacc0', B.dacc)


def phase4(K, B):
    S = B.S
    gl, sl, trA, trB = S['gl'], S['sl'], S['trA'], S['trB']
    mmA, vecA, scaA = S['mmA'], S['vecA'], S['scaA']
    cc, stats, statB = B.cc, B.stats, B.statB
    psA, psB = B.psA, B.psB
    K.op(SC, lambda e: e.activation(B.tiny[0:1, 30:31], B.cst[0:1, 8:9],
                                    AF.Sqrt), inc=scaA)
    oTn1 = B.bufB(A_OTN, HPH * R)
    oTn0 = B.bufB(FR + 0, HPH * R)
    woh = B.bufB(FR + 4096, 16 * KB)
    xq = B.bufF(0, 16384)
    x2bf = B.bufB(32768 + 0, D)
    trash = B.bufB(33792, D)
    x2T = B.bufB(36864, 16 * R)
    B.x2T = x2T

    K.dma(G, oTn0, B.otn0_d, inc=gl)
    for s8 in range(NSUB):
        K.dma(SY, xq[:, s8 * D:(s8 + 1) * D],
              B.xq_d[s8 * 128:(s8 + 1) * 128, :],
              inc=sl if s8 == 7 else None)
    xq_ld = sl.c
    oTn0_ld = gl.c
    wos0 = B.bufF(33792, 1024)
    wos1 = B.bufF(34816, 1024)
    ev_d = {-2: 0, -1: 0}
    wo_mm_prev = 0
    woc_d = {-2: 0, -1: 0}
    for half in range(2):
        for c in range(16):
            st_ = wos0 if c % 2 == 0 else wos1
            K.wait(SY, scaA, woc_d[half * 16 + c - 2])
            K.dma(SY, st_, B.wo_d[c * 128:(c + 1) * 128,
                                  half * KB:(half + 1) * KB], inc=sl)
            K.wait(SC, sl, sl.c)
            if half == 1 and c == 0:
                K.wait(SC, mmA, wo_mm_prev)
            K.op(SC, lambda e, c=c, st_=st_: e.activation(
                woh[:, c * KB:(c + 1) * KB], st_, AF.Copy), inc=scaA)
            woc_d[half * 16 + c] = scaA.c
        wo_ld = scaA.c
        for s8 in range(NSUB):
            for db in range(2):
                k_ = half * 16 + s8 * 2 + db
                ps = psA if k_ % 2 == 0 else psB
                if s8 == 0 and db == 0:
                    K.wait(TE, scaA, wo_ld)
                    K.wait(TE, gl, oTn0_ld)
                K.wait(TE, vecA, ev_d[k_ - 2])
                for hg in range(16):
                    src = oTn0 if hg < 8 else oTn1
                    K.mm(ps[:, 0:512],
                         src[:, (hg % 8) * R + s8 * 128:
                             (hg % 8) * R + (s8 + 1) * 128],
                         woh[:, hg * KB + db * 512: hg * KB + (db + 1) * 512],
                         start=(hg == 0), stop=(hg == 15),
                         inc=mmA if hg == 15 else None)
                K.wait(VE, mmA, mmA.c)
                if k_ == 0:
                    K.wait(VE, sl, xq_ld)
                sl_ = xq[:, s8 * D + half * KB + db * 512:
                         s8 * D + half * KB + (db + 1) * 512]
                K.op(VE, lambda e, ps=ps, sl_=sl_: e.tensor_add(
                    sl_, ps[:, 0:512], sl_), inc=vecA)
                ev_d[k_] = vecA.c
        wo_mm_prev = mmA.c
    K.drain(VE)
    y_done = vecA.c
    for s8 in range(NSUB):
        K.wait(VE, vecA, y_done)
        K.op(VE, lambda e, s8=s8: e.reduce_sum(
            stats[:, 32 + s8:33 + s8], xq[:, s8 * D:(s8 + 1) * D], AX.X),
            inc=vecA)
        K.wait(SC, vecA, vecA.c)
        K.op(SC, lambda e, s8=s8: e.activation(
            trash, xq[:, s8 * D:(s8 + 1) * D], AF.Square,
            accum_out=statB[:, 32 + s8:33 + s8]), inc=scaA)
    K.wait(VE, scaA, scaA.c)
    ln_chain(K, B, 32)
    ap_d = {}
    x2tr = {}
    for s8 in range(NSUB):
        K.op(VE, lambda e, s8=s8: e.tensor_scalar(
            xq[:, s8 * D:(s8 + 1) * D], xq[:, s8 * D:(s8 + 1) * D],
            stats[:, 40 + s8:41 + s8], statB[:, 40 + s8:41 + s8],
            ALU.subtract, ALU.mult), inc=vecA)
        ap_d[s8] = vecA.c
        K.wait(G, vecA, ap_d[s8])
        K.dma(G, B.x2_d[:, s8 * D:(s8 + 1) * D], xq[:, s8 * D:(s8 + 1) * D],
              inc=gl)
        K.wait(SC, vecA, ap_d[s8])
        K.wait(SC, trB, x2tr.get(s8 - 1, 0))
        K.op(SC, lambda e, s8=s8: e.activation(
            x2bf, xq[:, s8 * D:(s8 + 1) * D], AF.Copy), inc=scaA)
        K.dma(SC, wtr(x2T, s8 * 128, 16, R, 128), x2bf, inc=trB,
              transpose=True)
        x2tr[s8] = trB.c
    B.p4_x2T = trB.c
    B.p4_x2sp = gl.c


def ln_chain(K, B, base):
    cc, stats, statB = B.cc, B.stats, B.statB
    vecA, scaA = B.S['vecA'], B.S['scaA']
    s_ = slice(base, base + 8)
    d_ = slice(base + 8, base + 16)
    K.drain(VE)
    K.op(VE, lambda e: e.tensor_scalar(
        stats[:, d_], stats[:, s_], cc(C_ID), None, ALU.mult))
    K.op(VE, lambda e: e.tensor_scalar(
        statB[:, s_], statB[:, s_], cc(C_ID), None, ALU.mult))
    K.drain(VE)
    K.op(VE, lambda e: e.tensor_mul(B.tiny[:, 24:32], stats[:, d_],
                                    stats[:, d_]))
    K.drain(VE)
    K.op(VE, lambda e: e.tensor_sub(statB[:, s_], statB[:, s_],
                                    B.tiny[:, 24:32]))
    K.drain(VE)
    K.op(VE, lambda e: e.tensor_scalar(
        statB[:, s_], statB[:, s_], cc(C_ONE), cc(C_EPSLN),
        ALU.mult, ALU.add), inc=vecA)
    K.wait(SC, vecA, vecA.c)
    K.op(SC, lambda e: e.activation(statB[:, d_], statB[:, s_], AF.Sqrt),
         inc=scaA)
    K.wait(VE, scaA, scaA.c)
    K.op(VE, lambda e: e.reciprocal(statB[:, d_], statB[:, d_]))
    K.drain(VE)
    K.op(VE, lambda e: e.nop(), inc=vecA)


def phase5(K, B):
    S = B.S
    gl, sl, trA, trB = S['gl'], S['sl'], S['trA'], S['trB']
    mmA, vecA, scaA = S['mmA'], S['vecA'], S['scaA']
    psA, psB, psC = B.psA, B.psB, B.psC
    K.op(SC, lambda e: e.activation(B.tiny[0:1, 30:31], B.cst[0:1, 8:9],
                                    AF.Silu), inc=scaA)
    x2T = B.x2T
    WgT = B.bufB(0, 16 * KB)
    WuT = B.bufB(8192, 16 * KB)
    WdT = B.bufB(16384, 8 * D)
    rawi = B.bufI(24576, 2 * D)
    deq = B.bufB(28672, D)
    hT = B.bufB(29696, NSUB * KB)
    sg = B.bufB(33792, 512)
    hnat = B.bufB(34048, 512)
    part = B.bufF(34304, 1024)

    hn_prevs = [0]
    deq_done = {-2: 0, -1: 0}
    tr_done = {-1: 0}
    gu_ev = {-2: 0, -1: 0}
    dn_ev = {-2: 0, -1: 0}
    dma_acc = {-2: 0, -1: 0}
    li = 0
    mm_macro_prev = 0
    for m in range(NSUB):
        K.wait(SC, mmA, mm_macro_prev)
        for wi in range(2):
            src_d = B.gw_d if wi == 0 else B.uw_d
            wdst = WgT if wi == 0 else WuT
            sc1 = B.gsc if wi == 0 else B.usc
            sc2 = B.gsc2 if wi == 0 else B.usc2
            for tt in range(8):
                it = m * 8 + tt
                pp = li % 2
                K.wait(G, vecA, deq_done[li - 2])
                K.dma(G, rawi[:, pp * D:(pp + 1) * D],
                      src_d[it * 128:(it + 1) * 128, :], inc=gl)
                K.wait(VE, gl, gl.c)
                K.wait(VE, trB, tr_done[li - 1])
                K.op(VE, lambda e, pp=pp, it=it, sc1=sc1, sc2=sc2:
                     e.tensor_scalar(deq, rawi[:, pp * D:(pp + 1) * D],
                                     sc1[:, it:it + 1], sc2[:, it:it + 1],
                                     ALU.mult, ALU.add), inc=vecA)
                deq_done[li] = vecA.c
                K.wait(SC, vecA, deq_done[li])
                K.dma(SC, wtr(wdst, tt * 128, 16, KB, 128), deq,
                      inc=trB, transpose=True)
                tr_done[li] = trB.c
                li += 1
        for ht in range(16):
            pp = li % 2
            K.wait(SY, vecA, deq_done[li - 2])
            K.dma(SY, rawi[:, pp * D: pp * D + KB],
                  B.dw_d[ht * 128:(ht + 1) * 128, m * KB:(m + 1) * KB],
                  inc=sl)
            K.wait(VE, sl, sl.c)
            K.wait(VE, trB, tr_done[li - 1])
            K.op(VE, lambda e, pp=pp, ht=ht: e.tensor_scalar(
                deq[:, 0:KB], rawi[:, pp * D: pp * D + KB],
                B.dsc[:, ht:ht + 1], B.dsc2[:, ht:ht + 1],
                ALU.mult, ALU.add), inc=vecA)
            deq_done[li] = vecA.c
            K.wait(SC, vecA, deq_done[li])
            K.dma(SC, wtr(WdT, ht * 128, 8, D, 128), deq[:, 0:KB],
                  inc=trB, transpose=True)
            tr_done[li] = trB.c
            li += 1
        w_done = trB.c
        htr = 0
        hn_prev = hn_prevs
        for s8 in range(NSUB):
            for ib in range(2):
                kk = m * 16 + s8 * 2 + ib
                ps = psA if kk % 2 == 0 else psB
                if s8 == 0 and ib == 0:
                    K.wait(TE, trB, w_done)
                    if m == 0:
                        K.wait(TE, trB, B.p4_x2T)
                K.wait(TE, scaA, gu_ev[kk - 2])
                for c in range(16):
                    K.mm(ps[:, 0:512],
                         x2T[:, c * R + s8 * 128: c * R + (s8 + 1) * 128],
                         WgT[:, c * KB + ib * 512: c * KB + (ib + 1) * 512],
                         start=(c == 0), stop=(c == 15))
                for c in range(16):
                    K.mm(ps[:, 512:1024],
                         x2T[:, c * R + s8 * 128: c * R + (s8 + 1) * 128],
                         WuT[:, c * KB + ib * 512: c * KB + (ib + 1) * 512],
                         start=(c == 0), stop=(c == 15),
                         inc=mmA if c == 15 else None)
                K.wait(SC, mmA, mmA.c)
                K.wait(SC, vecA, hn_prev[0])
                K.op(SC, lambda e, ps=ps: e.activation(
                    sg, ps[:, 0:512], AF.Silu), inc=scaA)
                K.wait(VE, scaA, scaA.c)
                if kk > 0:
                    K.wait(VE, trB, htr)
                K.op(VE, lambda e, ps=ps: e.tensor_mul(
                    hnat, ps[:, 512:1024], sg), inc=vecA)
                hn_ok = vecA.c
                hn_prev[0] = vecA.c
                K.wait(SC, vecA, hn_ok)
                K.op(SC, lambda e: e.nop(), inc=scaA)
                gu_ev[kk] = scaA.c
                K.dma(SC, wtr(hT, ib * 4 * KB + s8 * 128, 4, KB, 128),
                      hnat, inc=trB, transpose=True)
                htr = trB.c
        for s8 in range(NSUB):
            for hb in range(4):
                kk = m * 32 + s8 * 4 + hb
                pc = psC[:, (kk % 2) * 512:(kk % 2) * 512 + 512]
                if s8 == 0 and hb == 0:
                    K.wait(TE, trB, htr)
                K.wait(TE, vecA, dn_ev[kk - 2])
                for ic in range(8):
                    K.mm(pc, hT[:, ic * KB + s8 * 128:
                                ic * KB + (s8 + 1) * 128],
                         WdT[:, ic * D + hb * 512: ic * D + (hb + 1) * 512],
                         start=(ic == 0), stop=(ic == 7),
                         inc=mmA if ic == 7 else None)
                K.wait(VE, mmA, mmA.c)
                K.wait(VE, gl, dma_acc[kk - 2])
                pslot = part[:, (kk % 2) * 512:(kk % 2) * 512 + 512]
                K.op(VE, lambda e, pc=pc, pslot=pslot: e.tensor_copy(
                    pslot, pc), inc=vecA)
                dn_ev[kk] = vecA.c
                K.wait(G, vecA, dn_ev[kk])
                K.dma(G, B.yffn_d[:, s8 * D + hb * 512:
                                  s8 * D + (hb + 1) * 512],
                      pslot, inc=gl, accum=(m > 0))
                dma_acc[kk] = gl.c
        mm_macro_prev = mmA.c
    B.p5_done = gl.c


def phase6(K, B):
    S = B.S
    gl, sl = S['gl'], S['sl']
    mmA, vecA, scaA = S['mmA'], S['vecA'], S['scaA']
    K.op(SC, lambda e: e.activation(B.tiny[0:1, 30:31], B.cst[0:1, 8:9],
                                    AF.Sqrt), inc=scaA)
    cc, stats, statB = B.cc, B.stats, B.statB
    x2 = B.bufF(0, 16384)
    yf = B.bufF(16384, 16384)
    trash = B.bufB(33792, D)
    K.wait(SY, gl, B.p5_done)
    K.wait(G, gl, B.p5_done)
    for s8 in range(NSUB):
        K.dma(G, x2[:, s8 * D:(s8 + 1) * D], B.x2_d[:, s8 * D:(s8 + 1) * D],
              inc=gl if s8 == 7 else None)
        K.dma(SY, yf[:, s8 * D:(s8 + 1) * D],
              B.yffn_d[:, s8 * D:(s8 + 1) * D],
              inc=sl if s8 == 7 else None)
    ld = sl.c
    ldg = gl.c
    K.wait(VE, sl, ld)
    K.wait(VE, gl, ldg)
    for s8 in range(NSUB):
        K.op(VE, lambda e, s8=s8: e.tensor_add(
            x2[:, s8 * D:(s8 + 1) * D], x2[:, s8 * D:(s8 + 1) * D],
            yf[:, s8 * D:(s8 + 1) * D]))
    K.drain(VE)
    for s8 in range(NSUB):
        K.op(VE, lambda e, s8=s8: e.reduce_sum(
            stats[:, 64 + s8:65 + s8], x2[:, s8 * D:(s8 + 1) * D], AX.X),
            inc=vecA)
        K.wait(SC, vecA, vecA.c)
        K.op(SC, lambda e, s8=s8: e.activation(
            trash, x2[:, s8 * D:(s8 + 1) * D], AF.Square,
            accum_out=statB[:, 64 + s8:65 + s8]), inc=scaA)
    K.wait(VE, scaA, scaA.c)
    ln_chain(K, B, 64)
    for s8 in range(NSUB):
        K.op(VE, lambda e, s8=s8: e.tensor_scalar(
            x2[:, s8 * D:(s8 + 1) * D], x2[:, s8 * D:(s8 + 1) * D],
            stats[:, 72 + s8:73 + s8], statB[:, 72 + s8:73 + s8],
            ALU.subtract, ALU.mult), inc=vecA)
        K.wait(SY, vecA, vecA.c)
        K.dma(SY, B.out_d[s8 * 128:(s8 + 1) * 128, :],
              x2[:, s8 * D:(s8 + 1) * D], inc=sl)
    K.wait(SY, sl, sl.c)


def program(K, B):
    for s_ in B.S.values():
        s_.c = 0
    K.dsem = {'gpsimd': B.S['gl'], 'sync': B.S['sl'], 'scalar': B.S['trB'],
              'vector': B.S['sl'], 'tensor': B.S['sl']}
    ph = B.S['ph']
    phase0(K, B)
    K.barrier(ph)
    if PHASE_LIMIT < 1:
        return
    phase1(K, B)
    K.barrier(ph)
    tap(K, B, 'krotT', B.bufB(A_KROT, T))
    tap(K, B, 'tiny', B.tiny)
    tap(K, B, 'ckvT', B.ckvT_d)
    tap(K, B, 'recD', B.recD)
    tap(K, B, 'denP', B.denP)
    if PHASE_LIMIT < 2:
        return
    phase2(K, B)
    K.barrier(ph)
    tap(K, B, 'qfT', B.qfT)
    tap(K, B, 'qaT', B.bufB(FR + 0, 12 * R))
    phase25(K, B)
    K.barrier(ph)
    tap(K, B, 'biasT', B.biasT)
    if PHASE_LIMIT < 3:
        return
    phase3(K, B)
    K.barrier(ph)
    tap(K, B, 'dacc', B.dacc)
    tap(K, B, 'kpT', B.bufB(FR + 8192, HPH * KB))
    tap(K, B, 'pT', B.bufB(FR + 16384, NSUB * KB))
    tap(K, B, 'maskA', B.maskA)
    tap(K, B, 'mask01t', B.mask01)
    tap(K, B, 'dbg', B.dbg)
    tap(K, B, 'vbf', B.bufB(FR + 12288, NSUB * KB))
    tap(K, B, 'biasT2', B.biasT)
    tap(K, B, 'oTn1', B.bufB(A_OTN, HPH * R))
    tap(K, B, 'otn0', B.otn0_d)
    if PHASE_LIMIT < 4:
        return
    phase4(K, B)
    K.barrier(ph)
    tap(K, B, 'x2T', B.x2T)
    tap(K, B, 'x2s', B.x2_d)
    if PHASE_LIMIT < 5:
        return
    phase5(K, B)
    K.barrier(ph)
    tap(K, B, 'yffn', B.yffn_d)
    if PHASE_LIMIT < 6:
        return
    phase6(K, B)


def build_nc(debug_taps=()):
    B = build(debug_taps)
    nc = B.nc
    with nc.Block() as block:
        for eng in ENGS:
            getattr(block, eng)(lambda _e, eng=eng: program(Ctx(nc, eng), B))
    return B


def _stage(inputs, core):
    x = np.asarray(inputs["L0_attn_hidden_states"])[0]
    idx = np.concatenate([np.arange(g * 1024 + core * 128,
                                    g * 1024 + core * 128 + 128)
                          for g in range(8)])
    d = {}
    d["xq"] = np.ascontiguousarray(x[idx])
    d["xk"] = np.ascontiguousarray(x)
    d["wqa"] = np.ascontiguousarray(np.asarray(inputs["L0_attn_Wq_a"]))
    d["wqb"] = np.ascontiguousarray(np.asarray(inputs["L0_attn_Wq_b"]))
    d["qnw"] = np.ascontiguousarray(
        np.asarray(inputs["L0_attn_q_norm_w"]).reshape(12, 128).T)
    d["wkva"] = np.ascontiguousarray(np.asarray(inputs["L0_attn_Wkv_a"]))
    d["wkvb"] = np.ascontiguousarray(np.asarray(inputs["L0_attn_Wkv_b"]))
    d["kvnw"] = np.ascontiguousarray(
        np.asarray(inputs["L0_attn_kv_norm_w"]).reshape(4, 128).T)
    d["wkr"] = np.ascontiguousarray(np.asarray(inputs["L0_attn_Wk_rope"]))
    d["wo"] = np.ascontiguousarray(np.asarray(inputs["L0_attn_Wo"]))
    d["gw"] = np.ascontiguousarray(np.asarray(inputs["L0_ffn_gate_w"],
                                              dtype=np.int32))
    d["uw"] = np.ascontiguousarray(np.asarray(inputs["L0_ffn_up_w"],
                                              dtype=np.int32))
    d["dw"] = np.ascontiguousarray(np.asarray(inputs["L0_ffn_down_w"],
                                              dtype=np.int32))
    d["gsc"] = np.ascontiguousarray(
        np.asarray(inputs["L0_ffn_gate_scale"]).reshape(64, 128).T)
    d["usc"] = np.ascontiguousarray(
        np.asarray(inputs["L0_ffn_up_scale"]).reshape(64, 128).T)
    d["dsc"] = np.ascontiguousarray(
        np.asarray(inputs["L0_ffn_down_scale"]).reshape(16, 128).T)
    inv = (1.0 / 10000.0 ** (np.arange(0, DR, 2, dtype=np.float32) / DR))
    ang = np.outer(np.arange(T, dtype=np.float32), inv)
    ckt, skt = np.cos(ang).astype(np.float32), np.sin(ang).astype(np.float32)
    d["ckt"], d["skt"] = ckt, skt
    d["cqr"] = np.ascontiguousarray(np.tile(ckt[idx], (1, H)))
    d["sqr"] = np.ascontiguousarray(np.tile(skt[idx], (1, H)))
    qi = 128 * core + np.arange(128)[:, None]
    kk = np.arange(KB)[None, :]
    d["mask01"] = (qi >= kk).astype(np.float32)
    cvals = np.zeros(16, np.float32)
    cvals[C_SQK] = SCALE_QK
    cvals[C_EPSR] = 1e-6
    cvals[C_I512] = 1.0 / 512.0
    cvals[C_EPSLN] = 1e-5
    cvals[C_ID] = 1.0 / 2048.0
    cvals[C_B512] = math.sqrt(512.0) * 1.05 * SCALE_QK
    cvals[C_I1536] = 1.0 / 1536.0
    cvals[C_M128] = -128.0
    cvals[C_ONE] = 1.0
    cvals[C_NEG1] = -1.0
    cvals[C_SHIFT] = EXP_SHIFT
    d["cst"] = np.tile(cvals[None, :], (128, 1))
    oneh = np.zeros((128, 256), np.float32)
    for h_ in range(16):
        oneh[:, h_ * 16 + h_] = 1.0
    d["oneh"] = oneh
    return d


_BUILD_CACHE = {}


def kernel(**inputs):
    key = tuple(sorted(DEBUG_TAPS)) + (PHASE_LIMIT,)
    if key not in _BUILD_CACHE:
        _BUILD_CACHE[key] = build_nc(
            tuple((n, s, d) for (n, s, d) in DEBUG_TAPS))
    B = _BUILD_CACHE[key]
    in_maps = [_stage(inputs, core) for core in range(8)]
    res = run_bass_kernel_spmd(B.nc, in_maps, core_ids=list(range(8)))
    out = np.zeros((T, D), np.float32)
    for core in range(8):
        oc = np.asarray(res.results[core]["out"])
        for g in range(8):
            out[g * 1024 + core * 128: g * 1024 + core * 128 + 128] = \
                oc[g * 128:(g + 1) * 128]
    if DEBUG_TAPS:
        kernel._tap_results = res.results
    return out.reshape(1, T, D)
